# version 03
# baseline (speedup 1.0000x reference)
import math
import numpy as np
import ml_dtypes
import concourse.bass as bass
import concourse.mybir as mybir
from concourse.ap import AP as _APc
from concourse.bass_utils import run_bass_kernel_spmd
from contextlib import ExitStack

F32 = mybir.dt.float32
BF16 = mybir.dt.bfloat16
AX = mybir.AxisListType.X
AF = mybir.ActivationFunctionType
ALU = mybir.AluOpType
NPBF = ml_dtypes.bfloat16

LI = 0.8 - 0.6 * math.exp(-0.3)
ONE_MINUS_LI = 1.0 - LI
NT = 16
EPS = 1e-5
VST = 2112

_ENGINES = ("sync", "tensor", "vector", "scalar", "gpsimd")


class _Item:
    __slots__ = ("eng", "fn", "deps", "pdeps", "amt", "need", "post", "key",
                 "small")


class _Prog:
    def __init__(self):
        self.items = []
        self.tails = {e: None for e in _ENGINES}

    def add(self, eng, fn, deps=(), pdeps=(), dma=False, grp=None, key=None,
            nopd=False, small=False):
        it = _Item()
        it.eng = eng
        it.fn = fn
        it.small = small
        it.key = key if key is not None else eng
        it.deps = [d for d in deps if d is not None]
        it.pdeps = [d for d in pdeps if d is not None]
        it.amt = 16 if dma else 1
        it.need = False
        it.post = 0
        if dma:
            prev = getattr(self, "_dma_last", {}).get(it.key)
            if prev is not None:
                it.pdeps.append(prev)
            self.__dict__.setdefault("_dma_last", {})[it.key] = it
        if eng == "tensor":
            lg = getattr(self, "_tgrp", None)
            if (grp is not None or lg is not None) and grp != lg                     and self.tails["tensor"] is not None:
                it.pdeps.append(self.tails["tensor"])
            self._tgrp = grp
        self.items.append(it)
        self.tails[eng] = it
        return it

    def barrier(self):
        tails = [self.tails[e] for e in _ENGINES if self.tails[e] is not None]
        out = {}
        for e in _ENGINES:
            out[e] = self.add(e, lambda eng: eng.nop(), deps=list(tails))
        return out

    def assign(self):
        for it in self.items:
            if it.amt == 16:
                it.need = True
            for d in it.deps:
                if d.eng != it.eng or d.key != it.key or \
                        (it.eng == "vector" and d.small):
                    d.need = True
            for d in it.pdeps:
                d.need = True
        cnt = {}
        for it in self.items:
            if it.need:
                cnt[it.key] = cnt.get(it.key, 0) + it.amt
            it.post = cnt.get(it.key, 0)

    def emit_engine(self, ename, eng, sems, hwm):
        h = hwm.setdefault(ename, {})
        for it in self.items:
            if it.eng != ename:
                continue
            waits = {}
            for d in it.deps:
                if (d.eng != ename or d.key != it.key
                        or (ename == "vector" and d.small)) \
                        and d.post > waits.get(d.key, 0):
                    waits[d.key] = d.post
            for d in it.pdeps:
                if d.post > waits.get(d.key, 0):
                    waits[d.key] = d.post
            for se, v in waits.items():
                if v > h.get(se, 0):
                    eng.wait_ge(sems[se], v)
                    h[se] = v
            inst = it.fn(eng)
            if it.need:
                inst.then_inc(sems[it.key], it.amt)


def _mkap(t_ap, off_elems, dims):
    b = t_ap[:, 0:1]
    return _APc(b.tensor, b.offset + off_elems, [list(b.ap[0])] + [list(d) for d in dims])


def _build(debug=False, phases="ABCD"):
    nc = bass.Bass("TRN2", target_bir_lowering=False,
                   detect_race_conditions=False, dynamic_dma_scratch_size=512)
    P = _Prog()

    x_d = nc.dram_tensor("x", [2048, 1024], F32, kind="ExternalInput").ap()
    wqkv_d = nc.dram_tensor("wqkv", [12, 128, 8, 256], F32, kind="ExternalInput").ap()
    wproj_d = nc.dram_tensor("wproj", [128, 8, 1024], F32, kind="ExternalInput").ap()
    lam_d = nc.dram_tensor("lamv", [4, 512], F32, kind="ExternalInput").ap()
    w1_d = nc.dram_tensor("w1t", [32, 1, 128, 8, 128], F32, kind="ExternalInput").ap()
    w2_d = nc.dram_tensor("w2t", [32, 1, 128, 8, 128], F32, kind="ExternalInput").ap()
    h1_d = nc.dram_tensor("h1t", [32, 4, 128, 8, 128], F32, kind="ExternalInput").ap()
    h2_d = nc.dram_tensor("h2t", [32, 4, 128, 8, 128], F32, kind="ExternalInput").ap()
    w3_d = nc.dram_tensor("w3t", [8, 4, 128, 8, 128], F32, kind="ExternalInput").ap()
    scl_d = nc.dram_tensor("scl", [128, 136], F32, kind="ExternalInput").ap()
    out_d = nc.dram_tensor("out", [2048, 1024], F32, kind="ExternalOutput").ap()
    x1_d = nc.dram_tensor("x1buf", [2048, 1024], F32, kind="Internal").ap()

    idn_d = nc.inline_tensor(np.eye(128, dtype=NPBF), "idn").ap()
    tri_np = np.asarray(np.arange(128)[None, :] >= np.arange(128)[:, None], dtype=NPBF)
    tri_d = nc.inline_tensor(tri_np, "tri").ap()

    taps = {}

    def tap(name, shape, dtype=F32):
        if not debug:
            return None
        taps[name] = nc.dram_tensor("tap_" + name, shape, dtype,
                                    kind="ExternalOutput").ap()
        return taps[name]

    xt_v = x_d.rearrange("(t p) c -> t p c", p=128)
    out_v = out_d.rearrange("(t p) c -> t p c", p=128)
    x1_v = x1_d.rearrange("(t p) c -> t p c", p=128)

    SB0 = 768

    def A(name, shape, dtype, offset):
        return nc.alloc_sbuf_tensor_at(name, shape, dtype, offset=offset + SB0)
    idn_s = A("idn_s", [128, 128], BF16, offset=0)
    tri_s = A("tri_s", [128, 128], BF16, offset=256)
    one_s = A("one_s", [128, 128], BF16, offset=512)
    lamc = A("lamc", [128, 1], F32, offset=768)
    bsm = A("bsm", [128, 1], F32, offset=800)
    nmq = A("nmq", [128, 1], F32, offset=832)
    nmk = A("nmk", [128, 1], F32, offset=864)
    smb = A("smb", [128, 64], F32, offset=896)
    pt_s = A("pt", [128, 2048], BF16, offset=1280)
    rbuf = A("rbuf", [128, 1024], F32, offset=5376)
    rb_bf = A("rb_bf", [128, 1024], BF16, offset=9472)
    x2sc = A("x2sc", [128, 1024], F32, offset=11520)
    t0_s = A("t0", [128, 512], F32, offset=11520)
    t1_s = A("t1", [128, 512], F32, offset=13568)
    outb2 = A("outb2", [128, 1024], F32, offset=15616)
    xt_d0 = A("xt_d0", [128, 1024], F32, offset=221504)
    xt_d1 = A("xt_d1", [128, 1024], F32, offset=216832)
    outb_a = A("outb_a", [128, 1024], F32, offset=5376)

    W0 = 19968
    wstg = A("wstg", [128, 2048], F32, offset=W0)
    wbf = A("wbf", [128, 4096], BF16, offset=W0 + 8192)
    qkvb = A("qkvb", [128, 512], BF16, offset=W0 + 16384)
    sqsc = A("sqsc", [128, 256], F32, offset=W0 + 17408)
    vth = A("vth", [128, 4096], BF16, offset=W0 + 18432)
    xt_a = A("xt_a", [128, 2048], F32, offset=W0)
    l1t_a = A("l1t_a", [128, 2048], BF16, offset=W0 + 8192)
    xt_c = A("xt_c", [128, 2048], F32, offset=W0)
    zsc_c = A("zsc_c", [128, 1024], F32, offset=W0 + 8192)
    x1t_c = A("x1t_c", [128, 2048], F32, offset=W0 + 12288)
    l2t_c = A("l2t_c", [128, 1024], BF16, offset=W0 + 20480)
    ub0s = [A("ub0a", [128, 512], F32, offset=212736),
            A("ub0b", [128, 512], F32, offset=W0)]
    ub1s = [A("ub1a", [128, 512], F32, offset=214784),
            A("ub1b", [128, 512], F32, offset=W0 + 2048)]
    dstg = A("dstg", [128, 4096], F32, offset=W0)
    dbf = A("dbf", [128, 4096], BF16, offset=W0 + 16384)
    sclb = A("sclb", [128, 136], F32, offset=220928)
    sqbf = A("sqbf", [128, 1024], BF16, offset=9472)


    L0 = W0 + 26624
    ln1f = A("ln1f", [128, 16384], BF16, offset=L0)
    ln2f = A("ln2f", [128, 16384], BF16, offset=L0)
    afm = A("afm", [128, 16384], BF16, offset=L0)
    ffb0 = A("ffb0", [128, 8192], BF16, offset=L0)
    ffb1 = A("ffb1", [128, 8192], BF16, offset=L0 + 16384)
    Q0 = L0 + 32768
    qt2 = A("qt2", [128, 32768], BF16, offset=Q0)
    y_s = A("y_s", [128, 16384], BF16, offset=Q0)
    wps = A("wps", [128, 8192], BF16, offset=Q0 + 32768)
    wpp = A("wpp", [128, 8192], BF16, offset=Q0 + 49152)
    ha_s = A("ha_s", [128, 32768], BF16, offset=Q0)
    K0 = Q0 + 65536
    kt2 = A("kt2", [128, 16384], BF16, offset=K0)
    wpstg = A("wpstg", [128, 8192], F32, offset=K0)
    hb_s = A("hb_s", [128, 32768], BF16, offset=K0)
    V0 = K0 + 32768
    vtk = A("vtk", [128, 8 * VST], BF16, offset=V0)
    dbgb = A("dbgb", [128, 2048], F32, offset=V0 + 35072)

    _sem_stack = ExitStack()
    sems = {e: _sem_stack.enter_context(nc.semaphore("sem_" + e)) for e in _ENGINES}

    def dma(dst, src, deps=(), ch="misc"):
        if ch not in sems:
            sems[ch] = _sem_stack.enter_context(nc.semaphore("sem_" + ch))
        return P.add("sync", lambda e, d=dst, s=src: e.dma_start(out=d, in_=s),
                     deps=deps, dma=True, key=ch)

    def vop(fn, deps=(), pdeps=(), small=False):
        return P.add("vector", fn, deps=deps, pdeps=pdeps, small=small)

    def sop(fn, deps=(), pdeps=()):
        return P.add("scalar", fn, deps=deps, pdeps=pdeps)

    def top(fn, deps=(), pdeps=(), grp=None, nopd=False):
        return P.add("tensor", fn, deps=deps, pdeps=pdeps, grp=grp, nopd=nopd)

    i_idn = dma(idn_s[:], idn_d)
    i_tri = dma(tri_s[:], tri_d)
    i_one = vop(lambda e: e.memset(one_s[:], 1.0))
    i_scl = dma(sclb[:], scl_d)
    i_lamdma = dma(rbuf[0:1, 0:1024].rearrange("p (a c) -> p a c", a=2), lam_d[0:2])
    i_lamdmb = dma(x2sc[0:1, 0:1024].rearrange("p (a c) -> p a c", a=2), lam_d[2:4])
    i_m1 = vop(lambda e: e.tensor_mul(outb2[0:1, 0:512], rbuf[0:1, 0:512],
                                      rbuf[0:1, 512:1024]), deps=[i_lamdma])
    i_m2 = vop(lambda e: e.tensor_mul(outb2[0:1, 512:1024], x2sc[0:1, 0:512],
                                      x2sc[0:1, 512:1024]), deps=[i_lamdmb])
    i_r1 = vop(lambda e: e.reduce_sum(smb[0:1, 20:21], outb2[0:1, 0:512], axis=AX),
               deps=[i_m1], small=True)
    i_r2 = vop(lambda e: e.reduce_sum(smb[0:1, 21:22], outb2[0:1, 512:1024], axis=AX),
               deps=[i_m2], small=True)
    i_e1 = sop(lambda e: e.activation(out=smb[0:1, 20:21], in_=smb[0:1, 20:21],
                                      func=AF.Exp), deps=[i_r1], pdeps=[i_r1])
    i_e2 = sop(lambda e: e.activation(out=smb[0:1, 21:22], in_=smb[0:1, 21:22],
                                      func=AF.Exp), deps=[i_r2], pdeps=[i_r2])
    i_ls = vop(lambda e: e.tensor_sub(smb[0:1, 22:23], smb[0:1, 20:21],
                                      smb[0:1, 21:22]), deps=[i_e1, i_e2],
               small=True)
    i_lv = vop(lambda e: e.tensor_scalar(smb[0:1, 22:23], smb[0:1, 22:23], 1.0, LI,
                                         ALU.mult, ALU.add), deps=[i_ls],
               small=True)
    i_lbf = vop(lambda e: e.tensor_copy(rb_bf[0:1, 0:1], smb[0:1, 22:23]),
                deps=[i_lv], small=True)
    i_eps = vop(lambda e: e.memset(smb[:, 30:31], EPS))

    psA = ExitStack()
    pqk0 = psA.enter_context(nc.psum_tensor("pqk0", [128, 512], F32))
    pqk1 = psA.enter_context(nc.psum_tensor("pqk1", [128, 512], F32))
    pat_a = psA.enter_context(nc.psum_tensor("pat_a", [128, 1024], BF16))
    pat_b = psA.enter_context(nc.psum_tensor("pat_b", [128, 1024], BF16))
    pav_a = psA.enter_context(nc.psum_tensor("pav_a", [128, 1024], BF16))
    pav_b = psA.enter_context(nc.psum_tensor("pav_b", [128, 1024], BF16))
    pnrm = psA.enter_context(nc.psum_tensor("pnrm", [128, 512], F32))

    i_lbc = top(lambda e: e.matmul(pnrm[:, 0:1], one_s[0:1, :], rb_bf[0:1, 0:1],
                                   start=True, stop=True), deps=[i_lbf, i_one])
    i_lamc = vop(lambda e: e.tensor_copy(lamc[:], pnrm[:, 0:1]), deps=[i_lbc],
                 small=True)

    i_nq0 = vop(lambda e: e.memset(nmq[:], 0.0))
    i_nk0 = vop(lambda e: e.memset(nmk[:], 0.0))

    def ln_stats(src_ap, rot, deps, sq_out):
        c = 8 * rot
        s1, s2 = smb[:, c:c + 1], smb[:, c + 1:c + 2]
        mu, var = smb[:, c + 2:c + 3], smb[:, c + 3:c + 4]
        rs, nb = smb[:, c + 4:c + 5], smb[:, c + 5:c + 6]
        mq = smb[:, c + 6:c + 7]
        i1 = vop(lambda e: e.reduce_sum(s1, src_ap, axis=AX), deps=deps,
                 small=True)
        i2 = sop(lambda e: e.activation(out=sq_out, in_=src_ap, func=AF.Square,
                                        accum_out=s2), deps=list(deps) + [i1])
        i3 = vop(lambda e: e.tensor_scalar(mu, s1, 1.0 / 1024.0, None, ALU.mult),
                 deps=[i1], pdeps=[i1], small=True)
        i4 = vop(lambda e: e.tensor_mul(mq, mu, mu), deps=[i3], small=True)
        i5 = vop(lambda e: e.scalar_tensor_tensor(var, s2, 1.0 / 1024.0, mq,
                                                  ALU.mult, ALU.subtract),
                 deps=[i2, i4], pdeps=[i2], small=True)
        i6 = sop(lambda e: e.activation(out=var, in_=var, func=AF.Ln,
                                        bias=smb[:, 30:31]),
                 deps=[i5], pdeps=[i5, i_eps])
        i7 = sop(lambda e: e.activation(out=rs, in_=var, func=AF.Exp, scale=-0.5),
                 deps=[i6])
        i8 = vop(lambda e: e.scalar_tensor_tensor(nb, mu, -1.0, rs, ALU.mult,
                                                  ALU.mult), deps=[i7], pdeps=[i7],
                 small=True)
        return i7, i8, rs, nb

    pat_banks = [pat_a, pat_b]
    pat_read = [None, None]
    pav_read = [None, None]
    gcnt = [0]

    xa_free = [None, None]
    l1_free = [None, None]
    for t in range(NT):
        rot = t % 2
        xslc = xt_a[:, 1024 * rot:1024 * (rot + 1)]
        i_x = dma(xslc, xt_v[t], deps=[xa_free[rot]], ch="x%d" % rot)
        i_rs, i_nb, rs, nb = ln_stats(xslc, rot, [i_x], x2sc[:])
        l1s = l1t_a[:, 1024 * rot:1024 * (rot + 1)]
        i_ap = sop(lambda e, s=xslc, d=l1s, r=rs, b=nb:
                   e.activation(out=d, in_=s, func=AF.Identity, scale=r, bias=b),
                   deps=[i_rs, i_nb, l1_free[rot]], pdeps=[i_rs, i_nb])
        xa_free[rot] = i_ap
        last_ev = None
        for k in range(8):
            bk = gcnt[0] % 2
            gcnt[0] += 1
            pb = pat_banks[bk]
            i_tr = top(lambda e, s=l1s, kk=k, d=pb[:, 0:128]:
                       e.transpose(d, s[:, 128 * kk:128 * (kk + 1)], idn_s[:]),
                       deps=[i_ap, i_idn, pat_read[bk]], grp=("t", 0, 0))
            i_ev = vop(lambda e, d=ln1f[:, 2048 * k + 128 * t:2048 * k + 128 * (t + 1)],
                       s=pb[:, 0:128]: e.tensor_copy(d, s), deps=[i_tr])
            pat_read[bk] = i_ev
            last_ev = i_ev
        l1_free[rot] = last_ev
    barA0 = P.barrier()

    if debug:
        dma(tap("ln1f", [128, 16384], BF16), ln1f[:], deps=[barA0["sync"]])

    tap_qkv = tap("qkv", [2048, 3072], BF16)
    tqv = tap_qkv.rearrange("(t p) c -> t p c", p=128) if debug else None

    i_wcast = [None, None]
    i_wdma_prev = None
    qkvb_free = [[], []]
    pqk_banks = [pqk0, pqk1]
    pqk_free = [None, None]
    nrm = smb[:, 24:28]
    nsc = smb[:, 28:29]
    nrm_prev = [None]
    q_ev_all = []
    k_ev_all = []
    v2_trs = {}
    step = [0]

    def norm_acc(qb, i_ev, acc, zdep):
        i_sq = sop(lambda e, s=qb: e.activation(out=sqsc[:], in_=s, func=AF.Square),
                   deps=[i_ev, nrm_prev[0]])
        i_rd = vop(lambda e: e.reduce_sum(
            nrm, sqsc[:].rearrange("p (j d) -> p j d", j=4), axis=AX), deps=[i_sq],
            small=True)
        i_mx = vop(lambda e: e.reduce_max(nsc, nrm, axis=AX), deps=[i_rd], small=True)
        i_ac = vop(lambda e, a=acc: e.tensor_max(a, a, nsc), deps=[i_mx, zdep],
                   small=True)
        nrm_prev[0] = i_ac
        return i_sq

    def extract_q(cs, tau, qb, i_ev):
        tq = tau % 8
        aa = tau // 8
        i_sq = norm_acc(qb, i_ev, nmq[:], i_nq0)
        bk = gcnt[0] % 2
        gcnt[0] += 1
        pb = pat_banks[bk]
        last = None
        for jj in range(4):
            last = top(lambda e, s=qb, j=jj, d=pb[0:64, 32 * jj:32 * jj + 32]:
                       e.transpose(d, s[0:32, 64 * j:64 * j + 64], idn_s[0:32, 0:32]),
                       deps=([i_ev, pat_read[bk]] if jj == 0 else []),
                       grp=("t", 0, 0))
        for jj in range(4):
            last = top(lambda e, s=qb, j=jj, d=pb[0:64, 128 + 32 * jj:160 + 32 * jj]:
                       e.transpose(d, s[64:96, 64 * j:64 * j + 64],
                                   idn_s[64:96, 64:96]),
                       deps=([i_ev] if jj == 0 else []), grp=("t", 64, 0))
        for jj in range(4):
            last = top(lambda e, s=qb, j=jj, d=pb[64:128, 256 + 64 * jj:320 + 64 * jj]:
                       e.transpose(d, s[0:64, 64 * j:64 * j + 64], idn_s[0:64, 0:64]),
                       deps=([i_ev] if jj == 0 else []), grp=("t", 0, 64))
        for jj in range(4):
            last = top(lambda e, s=qb, j=jj, d=pb[64:128, 512 + 64 * jj:576 + 64 * jj]:
                       e.transpose(d, s[64:128, 64 * j:64 * j + 64],
                                   idn_s[64:128, 64:128]),
                       deps=([i_ev] if jj == 0 else []), grp=("t", 64, 64))
        outs = []
        boff = 32 * aa + 4 * cs
        for pr in range(2):
            bo = (2 * tq + pr) * 2048 + boff
            qe = qt2[0:64, 0:1]
            de = _APc(qe.tensor, qe.offset + bo, [list(qe.ap[0]), [1, 4], [64, 32]])
            pe_ = pb[0:64, 0:1]
            se = _APc(pe_.tensor, pe_.offset + 128 * pr, [list(pe_.ap[0]), [32, 4], [1, 32]])
            i_e = vop(lambda e, s=se, d=de: e.tensor_copy(d, s), deps=[last])
            qo = qt2[64:128, 0:1]
            do2 = _APc(qo.tensor, qo.offset + bo, [list(qo.ap[0]), [1, 4], [64, 32]])
            po_ = pb[64:128, 0:1]
            so2 = _APc(po_.tensor, po_.offset + 256 + 256 * pr + 32,
                       [list(po_.ap[0]), [64, 4], [1, 32]])
            i_o = vop(lambda e, s=so2, d=do2: e.tensor_copy(d, s), deps=[last])
            pat_read[bk] = i_o
            outs.extend([i_e, i_o])
        q_ev_all.extend(outs)
        return outs + [i_sq]

    def extract_k(cs, tau, qb, i_ev):
        n = tau % 8
        aa = tau // 8
        i_sq = norm_acc(qb, i_ev, nmk[:], i_nk0)
        bk = gcnt[0] % 2
        gcnt[0] += 1
        pb = pat_banks[bk]
        last = None
        for jj in range(4):
            last = top(lambda e, s=qb, j=jj, d=pb[0:64, 128 * jj:128 * (jj + 1)]:
                       e.transpose(d, s[:, 64 * j:64 * j + 64], idn_s[:]),
                       deps=([i_ev, pat_read[bk]] if jj == 0 else []),
                       grp=("t", 0, 0))
        for jj in range(4):
            last = top(lambda e, s=qb, j=jj, d=pb[64:128, 128 * jj:128 * (jj + 1)]:
                       e.transpose(d, s[:, 64 * j:64 * j + 64], idn_s[:]),
                       deps=([i_ev] if jj == 0 else []), grp=("t", 0, 64))
        outs = []
        for jj in range(4):
            jp = 4 * (cs - 8) + jj
            dst = _mkap(kt2, n * 2048 + 8 * aa + jp, [[16, 128]])
            i_ev2 = vop(lambda e, s=pb[:, 128 * jj:128 * (jj + 1)], d=dst:
                        e.tensor_copy(d, s), deps=[last])
            outs.append(i_ev2)
        pat_read[bk] = outs[-1]
        k_ev_all.extend(outs)
        return outs + [i_sq]

    def extract_v1(csv, tau, qb, i_ev, vslot, slot_dep):
        aa = tau // 8
        outs = []
        for jj in range(4):
            jp = 4 * (csv - 10) + jj
            bk = gcnt[0] % 2
            gcnt[0] += 1
            pb = pat_banks[bk]
            i_t = top(lambda e, s=qb, j=jj, d=pb[64:128, 0:128]:
                      e.transpose(d, s[:, 64 * j:64 * j + 64], idn_s[:]),
                      deps=[i_ev, pat_read[bk]], grp=("t", 0, 64))
            vb = vth[64:128, 2048 * vslot:2048 * vslot + 1]
            dst = _APc(vb.tensor, vb.offset + 8 * aa + jp,
                       [list(vb.ap[0]), [16, 128]])
            i_ev2 = vop(lambda e, s=pb[64:128, 0:128], d=dst: e.tensor_copy(d, s),
                        deps=[i_t, slot_dep])
            pat_read[bk] = i_ev2
            outs.append(i_ev2)
        return outs

    def extract_v2(n, vslot, dep):
        outs = []
        for b in range(16):
            bk = b % 2
            pb = [pav_a, pav_b][bk]
            i_t = top(lambda e, s=vth[64:128, 2048 * vslot + 128 * b:
                                      2048 * vslot + 128 * (b + 1)],
                      d=pb[:, 0:64]: e.transpose(d, s, idn_s[64:128, 64:128]),
                      deps=[dep, pav_read[bk]], grp=("t", 64, 0))
            i_e = vop(lambda e, s=pb[:, 0:64],
                      d=vtk[:, n * VST + 128 * b + 64:n * VST + 128 * b + 128]:
                      e.tensor_copy(d, s), deps=[i_t, i_vones])
            pav_read[bk] = i_e
            v2_trs[n] = i_t
            outs.append(i_e)
        return outs

    i_vz = vop(lambda e: e.memset(vtk[:], 0.0))
    _ones_ap = _mkap(vtk, 0, [[VST, 8], [128, 17]])
    i_vones = vop(lambda e: e.memset(_ones_ap, 1.0), deps=[i_vz])

    def qkv_mm(wslc, tau, wdep, extra_mm_deps=()):
        par = step[0] % 2
        pq = pqk_banks[par]
        mm_last = None
        for k in range(8):
            mm_last = top(
                lambda e, o=pq[:, 0:256],
                l=ln1f[:, 2048 * k + 128 * tau:2048 * k + 128 * (tau + 1)],
                r=wslc[:, 256 * k:256 * (k + 1)], st=(k == 0), sp=(k == 7):
                e.matmul(o, l, r, start=st, stop=sp),
                deps=(([wdep, pqk_free[par], barA0["tensor"]] + list(extra_mm_deps))
                      if k == 0 else []))
        qsl = qkvb[:, 256 * par:256 * (par + 1)]
        i_ev = vop(lambda e, d=qsl, s=pq[:, 0:256]: e.tensor_copy(d, s),
                   deps=[mm_last] + qkvb_free[par])
        pqk_free[par] = i_ev
        step[0] += 1
        return qsl, i_ev, par

    for cs in range(10):
        slot = cs % 2
        wsl = wbf[:, 2048 * slot:2048 * (slot + 1)]
        d1 = dma(wstg[:].rearrange("p (k c) -> p k c", k=8), wqkv_d[cs],
                 deps=[i_wcast[1 - slot] if cs > 0 else barA0["sync"], i_wdma_prev],
                 ch="w")
        i_wdma_prev = d1
        c1 = vop(lambda e, d=wsl: e.tensor_copy(d, wstg[:]), deps=[d1])
        i_wcast[slot] = c1
        for tau in range(NT):
            qsl, i_ev, par = qkv_mm(wsl, tau, c1)
            i_tp = None
            if debug:
                i_tp = dma(tqv[tau][:, 256 * cs:256 * (cs + 1)], qsl, deps=[i_ev],
                           ch="tap")
            if cs < 8:
                readers = extract_q(cs, tau, qsl, i_ev)
            else:
                readers = extract_k(cs, tau, qsl, i_ev)
            qkvb_free[par] = readers[-2:] + ([i_tp] if i_tp else [])

    d10 = dma(wstg[:].rearrange("p (k c) -> p k c", k=8), wqkv_d[10],
              deps=[i_wcast[1], i_wdma_prev], ch="w")
    c10 = vop(lambda e: e.tensor_copy(wbf[:, 0:2048], wstg[:]), deps=[d10, i_wcast[0]])
    d11 = dma(wstg[:].rearrange("p (k c) -> p k c", k=8), wqkv_d[11], deps=[c10],
              ch="w")
    c11 = vop(lambda e: e.tensor_copy(wbf[:, 2048:4096], wstg[:]), deps=[d11])
    wc = {10: c10, 11: c11}
    v1_done = {}
    v2_done = {}
    for tau in [x for n_ in range(8) for x in (n_, n_ + 8)]:
        n = tau % 8
        aa = tau // 8
        vslot = n % 2
        slot_dep = v2_trs.get(n - 2)
        for csv in (10, 11):
            wsl = wbf[:, 2048 * (csv - 10):2048 * (csv - 9)]
            qsl, i_ev, par = qkv_mm(wsl, tau, wc[csv])
            i_tp = None
            if debug:
                i_tp = dma(tqv[tau][:, 256 * csv:256 * (csv + 1)], qsl, deps=[i_ev],
                           ch="tap")
            outs = extract_v1(csv, tau, qsl, i_ev, vslot, slot_dep)
            v1_done.setdefault(n, []).extend(outs)
            qkvb_free[par] = outs[-1:] + ([i_tp] if i_tp else [])
        if aa == 1:
            v2_done[n] = extract_v2(n, vslot, v1_done[n][-1])
    v2_all = [it for n_ in range(8) for it in v2_done[n_]]

    i_cq = vop(lambda e: e.tensor_copy(rb_bf[:, 2:3], nmq[:]), deps=[nrm_prev[0]],
               small=True)
    i_ck = vop(lambda e: e.tensor_copy(rb_bf[:, 3:4], nmk[:]), deps=[nrm_prev[0]],
               small=True)
    i_tq = top(lambda e: e.transpose(pat_a[0:1, 0:128], rb_bf[:, 2:3], idn_s[:]),
               deps=[i_cq, pat_read[0]], grp=("t", 0, 0))
    i_tk = top(lambda e: e.transpose(pat_b[0:1, 0:128], rb_bf[:, 3:4], idn_s[:]),
               deps=[i_ck, pat_read[1]], grp=("t", 0, 0))
    i_qm = vop(lambda e: e.reduce_max(smb[0:1, 16:17], pat_a[0:1, 0:128], axis=AX),
               deps=[i_tq], small=True)
    i_km = vop(lambda e: e.reduce_max(smb[0:1, 17:18], pat_b[0:1, 0:128], axis=AX),
               deps=[i_tk], small=True)
    i_qk = vop(lambda e: e.tensor_mul(smb[0:1, 18:19], smb[0:1, 16:17],
                                      smb[0:1, 17:18]), deps=[i_qm, i_km],
               small=True)
    i_s40 = vop(lambda e: e.tensor_copy(smb[0:1, 40:41], smb[0:1, 18:19]),
                deps=[i_qk])
    i_s41 = vop(lambda e: e.tensor_copy(smb[0:1, 41:42], smb[0:1, 16:17]),
                deps=[i_qk])
    i_lnm = sop(lambda e: e.activation(out=smb[0:1, 18:19], in_=smb[0:1, 18:19],
                                       func=AF.Ln), deps=[i_qk, i_s40],
                pdeps=[i_qk, i_s40])
    i_s42 = sop(lambda e: e.copy(smb[0:1, 42:43], smb[0:1, 18:19]), deps=[i_lnm])
    i_exm = sop(lambda e: e.activation(out=smb[0:1, 18:19], in_=smb[0:1, 18:19],
                                       func=AF.Exp, scale=0.5), deps=[i_lnm, i_s42])
    i_bs = vop(lambda e: e.tensor_scalar(smb[0:1, 19:20], smb[0:1, 18:19],
                                         -1.03 / 8.0, -0.1, ALU.mult, ALU.add),
               deps=[i_exm], pdeps=[i_exm], small=True)
    i_bbf = vop(lambda e: e.tensor_copy(rb_bf[0:1, 4:5], smb[0:1, 19:20]),
                deps=[i_bs], small=True)
    i_bbc = top(lambda e: e.matmul(pnrm[:, 1:2], one_s[0:1, :], rb_bf[0:1, 4:5],
                                   start=True, stop=True), deps=[i_bbf, i_lamc])
    i_bsm = vop(lambda e: e.tensor_copy(bsm[:], pnrm[:, 1:2]), deps=[i_bbc],
                small=True)

    if debug:
        dma(tap("qt2", [128, 32768], F32) if False else tap("qt2b", [128, 32768], BF16),
            qt2[:], deps=q_ev_all[-1:], ch="tap")
        dma(tap("kt2b", [128, 16384], BF16), kt2[:], deps=k_ev_all[-1:], ch="tap")
        dma(tap("vtkb", [128, 8 * VST], BF16), vtk[:], deps=v2_all[-1:], ch="tap")
        dma(tap("bsm", [128, 1]), bsm[:], deps=[i_bsm], ch="tap")
        dma(tap("smb", [128, 64]), smb[:], deps=[i_bsm], ch="tap")
        dma(tap("nmq", [128, 1]), nmq[:], deps=[i_bsm], ch="tap")
        dma(tap("nmk", [128, 1]), nmk[:], deps=[i_bsm], ch="tap")
        dma(tap("lamc", [128, 1]), lamc[:], deps=[i_lamc], ch="tap")

    barA = P.barrier()
    psA.close()

    if "B" not in phases:
        P.add("sync", lambda e: e.nop(), deps=[barA["sync"]])
        return _finish(nc, P, sems, taps)

    psB = ExitStack()
    pS = [psB.enter_context(nc.psum_tensor("ps0", [128, 1024], F32)),
          psB.enter_context(nc.psum_tensor("ps1", [128, 1024], F32))]
    pC = psB.enter_context(nc.psum_tensor("pc", [128, 1024], F32))
    pU0 = psB.enter_context(nc.psum_tensor("pu0", [128, 512], F32))
    pU1 = psB.enter_context(nc.psum_tensor("pu1", [128, 512], F32))

    iz0 = vop(lambda e: e.memset(pS[0][:], 0.0), deps=[barA["vector"]])
    iz1 = vop(lambda e: e.memset(pS[1][:], 0.0), deps=[iz0])
    pS_read = [iz1, iz1]
    pt_read = [None, None]
    pU_read = [None, None]
    ub_free = [[None, None], [None, None]]
    pC_read = [None]

    for hp in range(16):
        n = hp // 2
        pr_lo = 0 if hp % 2 == 0 else 64
        l_row = 64 if hp % 2 == 0 else 0
        vcol = 64 if hp % 2 == 0 else 0
        for c in range(4):
            nb_ = 4 * c + 4
            av0_last = av1_last = None
            for b in range(nb_):
                sb = max(0, 128 * (b - 4 * c))
                par = b % 2
                sreg = pS[par]
                i_s0 = top(lambda e, o=sreg[:, sb:512],
                           l=kt2[0:64, n * 2048 + 128 * b:n * 2048 + 128 * (b + 1)],
                           r=qt2[0:64, hp * 2048 + 512 * c + sb:
                                 hp * 2048 + 512 * (c + 1)]:
                           e.matmul(o, l, r, start=True, stop=True,
                                    tile_position=(0, 0)),
                           deps=[pS_read[par], barA["tensor"]])
                i_s1 = top(lambda e, o=sreg[:, 512 + sb:1024],
                           l=kt2[64:128, n * 2048 + 128 * b:n * 2048 + 128 * (b + 1)],
                           r=qt2[64:128, hp * 2048 + 512 * c + sb:
                                 hp * 2048 + 512 * (c + 1)]:
                           e.matmul(o, l, r, start=True, stop=True,
                                    tile_position=(64, 0)), deps=[],
                           grp=("rg", 64), nopd=True)
                ptsl = pt_s[:, 1024 * par:1024 * (par + 1)]
                i_ex = sop(lambda e, o=ptsl, s=sreg:
                           e.activation(out=o, in_=s[:], func=AF.Exp, scale=0.125,
                                        bias=bsm[:, 0:1]),
                           deps=[i_s1, pt_read[par], i_bsm], pdeps=[i_bsm])
                pS_read[par] = i_ex
                dep_av = i_ex
                if b >= 4 * c:
                    ll = 128 * (b - 4 * c)
                    i_mk0 = vop(lambda e, o=ptsl[:, ll:ll + 128]:
                                e.tensor_mul(o, o, tri_s[:]), deps=[i_ex, i_tri])
                    i_mk1 = vop(lambda e, o=ptsl[:, 512 + ll:512 + ll + 128]:
                                e.tensor_mul(o, o, tri_s[:]), deps=[i_mk0])
                    dep_av = i_mk1
                vap = vtk[:, n * VST + 128 * b + vcol:n * VST + 128 * b + vcol + 128]
                i_a0 = top(lambda e, o=pU0[:, sb:512], l=vap, r=ptsl[:, sb:512],
                           st=(b == 0), sp=(b == nb_ - 1):
                           e.matmul(o, l, r, start=st, stop=sp),
                           deps=[dep_av] + ([pU_read[0]] if b == 0 else []))
                i_a1 = top(lambda e, o=pU1[:, sb:512], l=vap,
                           r=ptsl[:, 512 + sb:1024],
                           st=(b == 0), sp=(b == nb_ - 1):
                           e.matmul(o, l, r, start=st, stop=sp),
                           deps=([pU_read[1]] if b == 0 else []))
                pt_read[par] = i_a1
                av0_last, av1_last = i_a0, i_a1
            tpar = (hp * 4 + c) % 2
            u0b, u1b = ub0s[tpar], ub1s[tpar]
            i_u0 = vop(lambda e, d=u0b: e.tensor_copy(d[:], pU0[:]),
                       deps=[av0_last, ub_free[tpar][0]])
            i_u1 = vop(lambda e, d=u1b: e.tensor_copy(d[:], pU1[:]),
                       deps=[av1_last, ub_free[tpar][1]])
            pU_read = [i_u0, i_u1]
            i_r0 = vop(lambda e, lr=l_row, s=u0b: e.reciprocal(
                rbuf[lr:lr + 1, 0:512], s[lr:lr + 1, :]),
                       deps=[i_u0], small=True)
            i_r1 = vop(lambda e, lr=l_row, s=u1b: e.reciprocal(
                rbuf[lr:lr + 1, 512:1024], s[lr:lr + 1, :]),
                       deps=[i_u1], small=True)
            i_c1 = sop(lambda e, lr=l_row: e.mul(rbuf[lr:lr + 1, 512:1024],
                                                 rbuf[lr:lr + 1, 512:1024],
                                                 lamc[lr:lr + 1, 0:1]),
                       deps=[i_r1], pdeps=[i_r1, i_lamc])
            i_snA = None
            if debug and hp == 2 and c == 3:
                i_snA = vop(lambda e, lr=l_row: e.tensor_copy(
                    dbgb[lr:lr + 1, 0:1024], rbuf[lr:lr + 1, 0:1024]),
                    deps=[i_r0, i_c1])
            i_cb = vop(lambda e, lr=l_row: e.tensor_copy(rb_bf[lr:lr + 1, 0:1024],
                                                         rbuf[lr:lr + 1, 0:1024]),
                       deps=[i_r0, i_c1, i_snA])
            i_b0 = top(lambda e, lr=l_row, pl=pr_lo: e.matmul(
                pC[pl:pl + 64, 0:512], one_s[lr:lr + 1, 0:64],
                rb_bf[lr:lr + 1, 0:512], start=True, stop=True),
                deps=[i_cb, pC_read[0]], grp=("bc", l_row, pr_lo))
            i_b1 = top(lambda e, lr=l_row, pl=pr_lo: e.matmul(
                pC[pl:pl + 64, 512:1024], one_s[lr:lr + 1, 64:128],
                rb_bf[lr:lr + 1, 512:1024], start=True, stop=True), deps=[],
                grp=("bc", l_row, pr_lo))
            i_ce = vop(lambda e, pl=pr_lo: e.tensor_copy(rbuf[pl:pl + 64, 0:1024],
                                                         pC[pl:pl + 64, 0:1024]),
                       deps=[i_b1])
            pC_read[0] = i_ce
            i_snB = None
            if debug and hp == 2 and c == 3:
                i_snB = vop(lambda e, pl=pr_lo: e.tensor_copy(
                    dbgb[pl:pl + 64, 1024:2048], rbuf[pl:pl + 64, 0:1024]),
                    deps=[i_ce])
                i_snC = vop(lambda e, s=u0b: e.tensor_copy(dbgb[64:128, 1024:1536],
                                                           s[64:128, :]), deps=[i_ce])
                dma(tap("dbgb", [128, 2048]), dbgb[:], deps=[i_snB, i_snC], ch="tap")
            i_t0 = vop(lambda e, pl=pr_lo, s=u0b: e.tensor_mul(
                t0_s[pl:pl + 64, :], s[pl:pl + 64, :], rbuf[pl:pl + 64, 0:512]),
                       deps=[i_ce, i_snB, i_u0])
            i_t1 = vop(lambda e, pl=pr_lo, s=u1b: e.tensor_mul(
                t1_s[pl:pl + 64, :], s[pl:pl + 64, :], rbuf[pl:pl + 64, 512:1024]),
                       deps=[i_ce, i_u1])
            i_af = vop(lambda e, pl=pr_lo,
                       d=afm[pr_lo:pr_lo + 64,
                             n * 2048 + 512 * c:n * 2048 + 512 * (c + 1)]:
                       e.tensor_sub(d, t0_s[pl:pl + 64, :], t1_s[pl:pl + 64, :]),
                       deps=[i_t0, i_t1])
            ub_free[tpar] = [i_t0, i_t1]

    barB = P.barrier()
    psB.close()
    if debug:
        dma(tap("afmb", [128, 16384], BF16), afm[:], deps=[barB["sync"]], ch="tap")

    if "C" not in phases:
        P.add("sync", lambda e: e.nop(), deps=[P.tails["sync"]])
        return _finish(nc, P, sems, taps)

    psC = ExitStack()
    pY = [psC.enter_context(nc.psum_tensor("py0", [128, 1024], F32)),
          psC.enter_context(nc.psum_tensor("py1", [128, 1024], F32))]
    pCT = psC.enter_context(nc.psum_tensor("pct", [128, 2048], BF16))

    i_wp = dma(wpstg[:].rearrange("p (k c) -> p k c", k=8), wproj_d,
               deps=[barB["sync"]], ch="w")
    i_wps = vop(lambda e: e.tensor_scalar(wps[:], wpstg[:], ONE_MINUS_LI, None,
                                          ALU.mult), deps=[i_wp, barB["vector"]])
    i_wpp = vop(lambda e: e.tensor_copy(wpp[:], wpstg[:]), deps=[i_wp])

    pY_read = [barB["vector"], barB["vector"]]
    y_ev = []
    for tau in range(NT):
        par = tau % 2
        mm_last = None
        for oc in range(2):
            for k in range(8):
                mm_last = top(
                    lambda e, o=pY[par][:, 512 * oc:512 * (oc + 1)],
                    l=afm[:, 2048 * k + 128 * tau:2048 * k + 128 * (tau + 1)],
                    r=wps[:, 1024 * k + 512 * oc:1024 * k + 512 * (oc + 1)],
                    st=(k == 0), sp=(k == 7):
                    e.matmul(o, l, r, start=st, stop=sp),
                    deps=([i_wps, pY_read[par], barB["tensor"]] if k == 0 else []))
        i_ye = vop(lambda e, d=y_s[:, 1024 * tau:1024 * (tau + 1)], s=pY[par][:]:
                   e.tensor_copy(d, s), deps=[mm_last])
        pY_read[par] = i_ye
        y_ev.append(i_ye)
    if debug:
        dma(tap("yb", [128, 16384], BF16), y_s[:], deps=[y_ev[-1]], ch="tap")

    if "C" in phases and "c1" in phases:
        P.add("sync", lambda e: e.nop(), deps=[P.tails["sync"]])
        return _finish(nc, P, sems, taps)

    xc_free = [None, None]
    x1_free = [None, None]
    pct_read = [None, None]
    ln2_ev = []
    x1dma = []
    x1d_prev = [None, None]
    zsc_free = [None]
    for st in range(NT):
        p_, ut = st // 8, st % 8
        par = st % 2
        mm_last = None
        for oc2 in range(2):
            for kk in range(8):
                mm_last = top(
                    lambda e, o=pY[par][:, 512 * oc2:512 * (oc2 + 1)],
                    l=y_s[:, 1024 * (8 * p_ + kk) + 128 * ut:
                          1024 * (8 * p_ + kk) + 128 * (ut + 1)],
                    r=wpp[:, 1024 * kk + 512 * oc2:1024 * kk + 512 * (oc2 + 1)],
                    st_=(kk == 0), sp=(kk == 7):
                    e.matmul(o, l, r, start=st_, stop=sp),
                    deps=([i_wpp, pY_read[par], y_ev[-1]] if kk == 0 else []))
        rot = st % 2
        i_rs, i_nb, rs, nb = ln_stats(pY[par][:], rot, [mm_last], x2sc[:])
        i_z = sop(lambda e, s=pY[par][:], r=rs, b=nb:
                  e.activation(out=zsc_c[:], in_=s, func=AF.Identity, scale=r,
                               bias=b),
                  deps=[i_rs, i_nb, zsc_free[0]], pdeps=[i_rs, i_nb])
        i_x = dma(xt_c[:, 1024 * rot:1024 * (rot + 1)], xt_v[st],
                  deps=[xc_free[rot], barB["sync"]], ch="x%d" % rot)
        x1sl = x1t_c[:, 1024 * rot:1024 * (rot + 1)]
        i_x1 = vop(lambda e, d=x1sl, xs=xt_c[:, 1024 * rot:1024 * (rot + 1)]:
                   e.tensor_add(d, zsc_c[:], xs),
                   deps=[i_z, i_x, x1_free[rot], x1d_prev[rot]])
        zsc_free[0] = i_x1
        pY_read[par] = i_x1
        xc_free[rot] = i_x1
        i_xd = dma(x1_v[st], x1sl, deps=[i_x1], ch="xo%d" % rot)
        x1dma.append(i_xd)
        x1d_prev[rot] = i_xd
        if "nl2" in phases:
            x1_free[rot] = i_xd
            ln2_ev.append(i_x1)
            continue
        i_rs2, i_nb2, rs2, nb2 = ln_stats(x1sl, rot, [i_x1], x2sc[:])
        i_l2 = sop(lambda e, s=x1sl, r=rs2, b=nb2:
                   e.activation(out=l2t_c[:], in_=s, func=AF.Identity,
                                scale=r, bias=b),
                   deps=[i_rs2, i_nb2, ln2_ev[-1] if ln2_ev else None],
                   pdeps=[i_rs2, i_nb2])
        half = pCT[:, 1024 * (st % 2):1024 * (st % 2 + 1)]
        trs = []
        for k in range(8):
            trs.append(top(lambda e, s=l2t_c[:, 128 * k:128 * (k + 1)],
                           d=half[:, 128 * k:128 * (k + 1)]:
                           e.transpose(d, s, idn_s[:]),
                           deps=([i_l2, pct_read[st % 2]] if k == 0 else []),
                           grp=("t", 0, 0)))
        dst = _mkap(ln2f, 8192 * (st // 8) + 128 * (st % 8), [[1024, 8], [1, 128]])
        srcp = half.rearrange("p (k c) -> p k c", k=8)
        i_l2e = vop(lambda e, d=dst, s=srcp: e.tensor_copy(d, s), deps=[trs[-1]])
        pct_read[st % 2] = i_l2e
        x1_free[rot] = i_l2e
        ln2_ev.append(i_l2e)

    barC = P.barrier()
    psC.close()
    if debug:
        dma(tap("ln2fb", [128, 16384], BF16), ln2f[:], deps=[barC["sync"]], ch="tap")
        dma(tap("x1", [2048, 1024]), x1_d, deps=[barC["sync"]], ch="tap")

    if "D" not in phases:
        P.add("sync", lambda e: e.nop(), deps=[P.tails["sync"]])
        return _finish(nc, P, sems, taps)

    psD = ExitStack()
    pDa = psD.enter_context(nc.psum_tensor("pda", [128, 1024], F32))
    pDb = psD.enter_context(nc.psum_tensor("pdb", [128, 1024], F32))
    pDc = psD.enter_context(nc.psum_tensor("pdc", [128, 1024], F32))
    pDT = psD.enter_context(nc.psum_tensor("pdt", [128, 2048], BF16))

    dstg_free = [None, None, None, None]
    dbf_free = [None, None, None, None]
    slab_i = [0]

    def stream_slab(src_ap, deps=()):
        si = slab_i[0]
        slab_i[0] += 1
        sl = si % 4
        st_ = dstg[:, 1024 * sl:1024 * (sl + 1)]
        bf_ = dbf[:, 1024 * sl:1024 * (sl + 1)]
        i_d = dma(st_.rearrange("p (k c) -> p k c", k=8), src_ap,
                  deps=list(deps) + [dstg_free[sl]], ch="fw%d" % sl)
        i_c = vop(lambda e, d=bf_, s=st_: e.tensor_copy(d, s),
                  deps=[i_d, dbf_free[sl]])
        dstg_free[sl] = i_c
        return bf_, i_c, sl

    scales = {"w1": sclb[:, 0:32], "w2": sclb[:, 32:64], "h1": sclb[:, 64:96],
              "h2": sclb[:, 96:128], "w3": sclb[:, 128:136]}
    ffbs = [ffb0, ffb1]
    out_done = []
    prev_w3_mm = [barC["tensor"]]
    prev_h2_mm = [barC["tensor"]]
    pDT_read = [None, None]
    for tt in range(2):
        l2base = 8192 * tt
        pDa_read = [barC["vector"], barC["vector"]]
        pDb_read = [barC["vector"], barC["vector"]]
        gu_free = [None, None, None, None]
        h0_last = None
        for m in range(32):
            b1, c1_, sl1 = stream_slab(w1_d[m, 0],
                                       deps=[barC["sync"]] if (tt == 0 and m < 2)
                                       else ())
            b2, c2_, sl2 = stream_slab(w2_d[m, 0],
                                       deps=[barC["sync"]] if (tt == 0 and m < 2)
                                       else ())
            mg_last = mu_last = None
            for tc in range(2):
                pi = tc
                mg = mu_ = None
                for k in range(8):
                    mg = top(lambda e, o=pDa[:, 512 * pi:512 * (pi + 1)],
                             l=b1[:, 128 * k:128 * (k + 1)],
                             r=ln2f[:, l2base + 1024 * k + 512 * tc:
                                    l2base + 1024 * k + 512 * (tc + 1)],
                             st=(k == 0), sp=(k == 7):
                             e.matmul(o, l, r, start=st, stop=sp),
                             deps=([c1_, pDa_read[pi], barC["tensor"]]
                                   if k == 0 else []))
                for k in range(8):
                    mu_ = top(lambda e, o=pDb[:, 512 * pi:512 * (pi + 1)],
                              l=b2[:, 128 * k:128 * (k + 1)],
                              r=ln2f[:, l2base + 1024 * k + 512 * tc:
                                     l2base + 1024 * k + 512 * (tc + 1)],
                              st=(k == 0), sp=(k == 7):
                              e.matmul(o, l, r, start=st, stop=sp),
                              deps=([c2_, pDb_read[pi]] if k == 0 else []))
                gq = pt_s[:, 512 * pi:512 * (pi + 1)]
                uq = pt_s[:, 1024 + 512 * pi:1024 + 512 * (pi + 1)]
                i_g = sop(lambda e, o=gq, s=pDa[:, 512 * pi:512 * (pi + 1)],
                          sc=scales["w1"][:, m:m + 1]:
                          e.activation(out=o, in_=s, func=AF.Silu, scale=sc),
                          deps=[mg, gu_free[pi], i_scl], pdeps=[i_scl])
                i_u = sop(lambda e, o=uq, s=pDb[:, 512 * pi:512 * (pi + 1)],
                          sc=scales["w2"][:, m:m + 1]:
                          e.activation(out=o, in_=s, func=AF.Identity, scale=sc),
                          deps=[mu_, gu_free[2 + pi]], pdeps=[i_scl])
                pDa_read[pi] = i_g
                pDb_read[pi] = i_u
                i_h0 = vop(lambda e, d=ha_s[:, 1024 * m + 512 * tc:
                                            1024 * m + 512 * (tc + 1)],
                           g=gq, u=uq: e.tensor_mul(d, g, u),
                           deps=[i_g, i_u] + ([prev_w3_mm[0]]
                                              if (m == 0 and tc == 0) else []))
                gu_free[pi] = i_h0
                gu_free[2 + pi] = i_h0
                h0_last = i_h0
                mg_last, mu_last = mg, mu_
            dbf_free[sl1] = mg_last
            dbf_free[sl2] = mu_last

        def big_layer(w_dram, scl_ap, src, dst, func, nm, nk, dep0, dst_free):
            evs = []
            pr_read = [dep0, dep0]
            last_mm = [None]
            for m in range(nm):
                slabs = []
                for q in range(nk // 8):
                    slabs.append(stream_slab(w_dram[m, q]))
                mmf = None
                for tc in range(2):
                    pi = tc
                    mm = None
                    for k in range(nk):
                        bq, cq, _ = slabs[k // 8]
                        mm = top(lambda e, o=pDc[:, 512 * pi:512 * (pi + 1)],
                                 l=bq[:, 128 * (k % 8):128 * (k % 8 + 1)],
                                 r=src[:, 1024 * k + 512 * tc:
                                       1024 * k + 512 * (tc + 1)],
                                 st=(k == 0), sp=(k == nk - 1):
                                 e.matmul(o, l, r, start=st, stop=sp),
                                 deps=(([cq, pr_read[pi], dep0] if k == 0 else
                                        ([cq] if k % 8 == 0 else []))))
                    i_o = sop(lambda e, o=dst[:, 1024 * m + 512 * tc:
                                              1024 * m + 512 * (tc + 1)],
                              s=pDc[:, 512 * pi:512 * (pi + 1)],
                              sc=scl_ap[:, m:m + 1], f=func:
                              e.activation(out=o, in_=s, func=f, scale=sc),
                              deps=[mm, dst_free], pdeps=[i_scl])
                    pr_read[pi] = i_o
                    evs.append(i_o)
                    mmf = mm
                for _, cq, slq in slabs:
                    dbf_free[slq] = mmf
                last_mm[0] = mmf
            return evs, last_mm[0]

        if debug and tt == 0:
            dma(tap("h0", [128, 32768], BF16), ha_s[:], deps=[h0_last], ch="tap")
        h1_ev, h1_mm = big_layer(h1_d, scales["h1"], ha_s, hb_s, AF.Gelu, 32, 32,
                                 h0_last, prev_h2_mm[0])
        if debug and tt == 0:
            dma(tap("h1", [128, 32768], BF16), hb_s[:], deps=[h1_ev[-1]], ch="tap")
        h2_ev, h2_mm = big_layer(h2_d, scales["h2"], hb_s, ha_s, AF.Gelu, 32, 32,
                                 h1_ev[-1], None)
        prev_h2_mm[0] = h2_mm
        if debug and tt == 0:
            dma(tap("h2", [128, 32768], BF16), ha_s[:], deps=[h2_ev[-1]], ch="tap")
        f_ev, w3_mm = big_layer(w3_d, scales["w3"], ha_s, ffbs[tt], AF.Identity,
                                8, 32, h2_ev[-1], None)
        if debug and tt == 0:
            dma(tap("ff", [128, 8192], BF16), ffbs[tt][:], deps=[f_ev[-1]], ch="tap")
        prev_w3_mm[0] = w3_mm

        xd_free = [None, None]
        ob_free = [None, None]
        x2_free = [None]
        outbs = [outb_a, outb2]
        for ss in range(8):
            st = 8 * tt + ss
            rot = ss % 2
            half = pDT[:, 1024 * rot:1024 * (rot + 1)]
            trs = []
            for k in range(8):
                trs.append(top(lambda e, s=ffbs[tt][:, 1024 * k + 128 * ss:
                                                    1024 * k + 128 * (ss + 1)],
                               d=half[:, 128 * k:128 * (k + 1)]:
                               e.transpose(d, s, idn_s[:]),
                               deps=([f_ev[-1], pDT_read[rot]] if k == 0 else []),
                               grp=("t", 0, 0)))
            i_rs, i_nb, rs, nb = ln_stats(half[:], rot, [trs[-1]], sqbf[:])
            i_z = sop(lambda e, s=half[:], r=rs, b=nb:
                      e.activation(out=x2sc[:], in_=s, func=AF.Identity,
                                   scale=r, bias=b),
                      deps=[i_rs, i_nb, x2_free[0]], pdeps=[i_rs, i_nb])
            xts = [xt_d0, xt_d1][rot]
            i_x = dma(xts[:], x1_v[st],
                      deps=[x1dma[st], xd_free[rot], barC["sync"]], ch="x%d" % rot)
            i_o = vop(lambda e, d=outbs[rot][:], xs=xts[:]:
                      e.tensor_add(d, x2sc[:], xs),
                      deps=[i_z, i_x, ob_free[rot]])
            x2_free[0] = i_o
            pDT_read[rot] = i_o
            xd_free[rot] = i_o
            i_od = dma(out_v[st], outbs[rot][:], deps=[i_o], ch="out%d" % rot)
            ob_free[rot] = i_od
            out_done.append(i_od)

    psD.close()
    P.add("sync", lambda e: e.nop(), deps=out_done)
    return _finish(nc, P, sems, taps)


def _finish(nc, P, sems, taps):
    P.assign()
    hwm = {}
    with nc.Block() as blk:
        @blk.sync
        def _(eng):
            P.emit_engine("sync", eng, sems, hwm)

        @blk.tensor
        def _(eng):
            P.emit_engine("tensor", eng, sems, hwm)

        @blk.vector
        def _(eng):
            P.emit_engine("vector", eng, sems, hwm)

        @blk.scalar
        def _(eng):
            P.emit_engine("scalar", eng, sems, hwm)

        @blk.gpsimd
        def _(eng):
            P.emit_engine("gpsimd", eng, sems, hwm)

    return nc, taps


def _host_layout(inputs):
    f32 = np.float32
    Wqkv = np.ascontiguousarray(np.asarray(inputs["L0_attn_Wqkv"], f32))
    wqkv_t = np.ascontiguousarray(
        Wqkv.reshape(8, 128, 12, 256).transpose(2, 1, 0, 3))
    wproj_t = np.ascontiguousarray(
        np.asarray(inputs["L0_attn_Wproj"], f32).reshape(8, 128, 1024)
        .transpose(1, 0, 2))
    lamv = np.ascontiguousarray(np.stack([
        np.asarray(inputs["L0_attn_lambda_q1"], f32),
        np.asarray(inputs["L0_attn_lambda_k1"], f32),
        np.asarray(inputs["L0_attn_lambda_q2"], f32),
        np.asarray(inputs["L0_attn_lambda_k2"], f32)], axis=0))

    def wtile(w, nm, nk):
        wt = np.asarray(w, f32).T
        t = wt.reshape(nk // 8, 8, 128, nm, 128).transpose(3, 0, 2, 1, 4)
        return np.ascontiguousarray(t)

    w1t = wtile(inputs["L0_ffn_w1_w"], 32, 8)
    w2t = wtile(inputs["L0_ffn_w2_w"], 32, 8)
    h1t = wtile(inputs["L0_ffn_h1_w"], 32, 32)
    h2t = wtile(inputs["L0_ffn_h2_w"], 32, 32)
    w3t = wtile(inputs["L0_ffn_w3_w"], 8, 32)
    scl = np.zeros((128, 136), f32)
    scl[:, 0:32] = np.asarray(inputs["L0_ffn_w1_s"], f32).reshape(32, 128).T
    scl[:, 32:64] = np.asarray(inputs["L0_ffn_w2_s"], f32).reshape(32, 128).T
    scl[:, 64:96] = np.asarray(inputs["L0_ffn_h1_s"], f32).reshape(32, 128).T
    scl[:, 96:128] = np.asarray(inputs["L0_ffn_h2_s"], f32).reshape(32, 128).T
    scl[:, 128:136] = np.asarray(inputs["L0_ffn_w3_s"], f32).reshape(8, 128).T
    return dict(wqkv=wqkv_t, wproj=wproj_t, lamv=lamv, w1t=w1t, w2t=w2t,
                h1t=h1t, h2t=h2t, w3t=w3t, scl=scl)


def kernel(**inputs):
    shared = _host_layout(inputs)
    x = np.asarray(inputs["L0_attn_x"], np.float32)
    nc, _ = _build(debug=False)
    in_maps = []
    for b in range(8):
        xu = np.ascontiguousarray(
            np.concatenate([x[b, 0::2], x[b, 1::2]], axis=0))
        m = {"x": xu}
        m.update(shared)
        in_maps.append(m)
    res = run_bass_kernel_spmd(nc, in_maps, core_ids=list(range(8)))
    out = np.empty((8, 2048, 1024), np.float32)
    for b in range(8):
        o = res.results[b]["out"]
        out[b, 0::2] = o[:1024]
        out[b, 1::2] = o[1024:]
    return out


# version 04
# speedup vs baseline: 1.2315x; 1.2315x over previous
import math
import numpy as np
import ml_dtypes
import concourse.bass as bass
import concourse.mybir as mybir
from concourse.ap import AP as _APc
from concourse.bass_utils import run_bass_kernel_spmd
from contextlib import ExitStack

F32 = mybir.dt.float32
BF16 = mybir.dt.bfloat16
AX = mybir.AxisListType.X
AF = mybir.ActivationFunctionType
ALU = mybir.AluOpType
NPBF = ml_dtypes.bfloat16

LI = 0.8 - 0.6 * math.exp(-0.3)
ONE_MINUS_LI = 1.0 - LI
NT = 16
EPS = 1e-5
VST = 2112

_ENGINES = ("sync", "tensor", "vector", "scalar", "gpsimd")


class _Item:
    __slots__ = ("eng", "fn", "deps", "pdeps", "amt", "need", "post", "key",
                 "small")


class _Prog:
    def __init__(self):
        self.items = []
        self.tails = {e: None for e in _ENGINES}

    def add(self, eng, fn, deps=(), pdeps=(), dma=False, grp=None, key=None,
            nopd=False, small=False):
        it = _Item()
        it.eng = eng
        it.fn = fn
        it.small = small
        it.key = key if key is not None else eng
        it.deps = [d for d in deps if d is not None]
        it.pdeps = [d for d in pdeps if d is not None]
        it.amt = 16 if dma else 1
        it.need = False
        it.post = 0
        if dma:
            prev = getattr(self, "_dma_last", {}).get(it.key)
            if prev is not None:
                it.pdeps.append(prev)
            self.__dict__.setdefault("_dma_last", {})[it.key] = it
        if eng == "tensor":
            lg = getattr(self, "_tgrp", None)
            if (grp is not None or lg is not None) and grp != lg                     and self.tails["tensor"] is not None:
                it.pdeps.append(self.tails["tensor"])
            self._tgrp = grp
        self.items.append(it)
        self.tails[eng] = it
        return it

    def barrier(self):
        tails = [self.tails[e] for e in _ENGINES if self.tails[e] is not None]
        out = {}
        for e in _ENGINES:
            out[e] = self.add(e, lambda eng: eng.nop(), deps=list(tails))
        return out

    def assign(self):
        for it in self.items:
            if it.amt == 16:
                it.need = True
            for d in it.deps:
                if d.eng != it.eng or d.key != it.key or \
                        (it.eng == "vector" and d.small):
                    d.need = True
            for d in it.pdeps:
                d.need = True
        cnt = {}
        for it in self.items:
            if it.need:
                cnt[it.key] = cnt.get(it.key, 0) + it.amt
            it.post = cnt.get(it.key, 0)

    def emit_engine(self, ename, eng, sems, hwm):
        h = hwm.setdefault(ename, {})
        for it in self.items:
            if it.eng != ename:
                continue
            waits = {}
            for d in it.deps:
                if (d.eng != ename or d.key != it.key
                        or (ename == "vector" and d.small)) \
                        and d.post > waits.get(d.key, 0):
                    waits[d.key] = d.post
            for d in it.pdeps:
                if d.post > waits.get(d.key, 0):
                    waits[d.key] = d.post
            for se, v in waits.items():
                if v > h.get(se, 0):
                    eng.wait_ge(sems[se], v)
                    h[se] = v
            inst = it.fn(eng)
            if it.need:
                inst.then_inc(sems[it.key], it.amt)


def _mkap(t_ap, off_elems, dims):
    b = t_ap[:, 0:1]
    return _APc(b.tensor, b.offset + off_elems, [list(b.ap[0])] + [list(d) for d in dims])


def _build(debug=False, phases="ABCD"):
    nc = bass.Bass("TRN2", target_bir_lowering=False,
                   detect_race_conditions=False, dynamic_dma_scratch_size=512)
    P = _Prog()

    x_d = nc.dram_tensor("x", [2048, 1024], F32, kind="ExternalInput").ap()
    wqkv_d = nc.dram_tensor("wqkv", [12, 128, 8, 256], F32, kind="ExternalInput").ap()
    wproj_d = nc.dram_tensor("wproj", [128, 8, 1024], F32, kind="ExternalInput").ap()
    lam_d = nc.dram_tensor("lamv", [4, 512], F32, kind="ExternalInput").ap()
    w1_d = nc.dram_tensor("w1t", [32, 1, 128, 8, 128], F32, kind="ExternalInput").ap()
    w2_d = nc.dram_tensor("w2t", [32, 1, 128, 8, 128], F32, kind="ExternalInput").ap()
    h1_d = nc.dram_tensor("h1t", [32, 4, 128, 8, 128], F32, kind="ExternalInput").ap()
    h2_d = nc.dram_tensor("h2t", [32, 4, 128, 8, 128], F32, kind="ExternalInput").ap()
    w3_d = nc.dram_tensor("w3t", [8, 4, 128, 8, 128], F32, kind="ExternalInput").ap()
    scl_d = nc.dram_tensor("scl", [128, 136], F32, kind="ExternalInput").ap()
    out_d = nc.dram_tensor("out", [2048, 1024], F32, kind="ExternalOutput").ap()
    x1_d = nc.dram_tensor("x1buf", [2048, 1024], F32, kind="Internal").ap()

    idn_d = nc.inline_tensor(np.eye(128, dtype=NPBF), "idn").ap()
    tri_np = np.asarray(np.arange(128)[None, :] >= np.arange(128)[:, None], dtype=NPBF)
    tri_d = nc.inline_tensor(tri_np, "tri").ap()

    taps = {}

    def tap(name, shape, dtype=F32):
        if not debug:
            return None
        taps[name] = nc.dram_tensor("tap_" + name, shape, dtype,
                                    kind="ExternalOutput").ap()
        return taps[name]

    xt_v = x_d.rearrange("(t p) c -> t p c", p=128)
    out_v = out_d.rearrange("(t p) c -> t p c", p=128)
    x1_v = x1_d.rearrange("(t p) c -> t p c", p=128)

    SB0 = 768

    def A(name, shape, dtype, offset):
        return nc.alloc_sbuf_tensor_at(name, shape, dtype, offset=offset + SB0)
    idn_s = A("idn_s", [128, 128], BF16, offset=0)
    tri_s = A("tri_s", [128, 128], BF16, offset=256)
    one_s = A("one_s", [128, 128], BF16, offset=512)
    lamc = A("lamc", [128, 1], F32, offset=768)
    bsm = A("bsm", [128, 1], F32, offset=800)
    nmq = A("nmq", [128, 1], F32, offset=832)
    nmk = A("nmk", [128, 1], F32, offset=864)
    smb = A("smb", [128, 64], F32, offset=896)
    pt_s = A("pt", [128, 2048], BF16, offset=1280)
    rbuf = A("rbuf", [128, 1024], F32, offset=5376)
    rb_bf = A("rb_bf", [128, 1024], BF16, offset=9472)
    x2sc = A("x2sc", [128, 1024], F32, offset=11520)
    t0_s = A("t0", [128, 512], F32, offset=11520)
    t1_s = A("t1", [128, 512], F32, offset=13568)
    outb2 = A("outb2", [128, 1024], F32, offset=15616)
    xt_d0 = A("xt_d0", [128, 1024], F32, offset=221504)
    xt_d1 = A("xt_d1", [128, 1024], F32, offset=216832)
    outb_a = A("outb_a", [128, 1024], F32, offset=5376)

    W0 = 19968
    wstg = A("wstg", [128, 2048], F32, offset=W0)
    wbf = A("wbf", [128, 4096], BF16, offset=W0 + 8192)
    qkvb = A("qkvb", [128, 512], BF16, offset=W0 + 16384)
    sqsc = A("sqsc", [128, 256], F32, offset=W0 + 17408)
    vth = A("vth", [128, 4096], BF16, offset=W0 + 18432)
    xt_a = A("xt_a", [128, 2048], F32, offset=W0)
    l1t_a = A("l1t_a", [128, 2048], BF16, offset=W0 + 8192)
    xt_c = A("xt_c", [128, 2048], F32, offset=W0)
    zsc_c = A("zsc_c", [128, 1024], F32, offset=W0 + 8192)
    x1t_c = A("x1t_c", [128, 2048], F32, offset=W0 + 12288)
    l2t_c = A("l2t_c", [128, 1024], BF16, offset=W0 + 20480)
    ub0s = [A("ub0a", [128, 512], F32, offset=212736),
            A("ub0b", [128, 512], F32, offset=W0)]
    ub1s = [A("ub1a", [128, 512], F32, offset=214784),
            A("ub1b", [128, 512], F32, offset=W0 + 2048)]
    dstg = A("dstg", [128, 4096], F32, offset=W0)
    dbf = A("dbf", [128, 4096], BF16, offset=W0 + 16384)
    sclb = A("sclb", [128, 136], F32, offset=220928)
    sqbf = A("sqbf", [128, 1024], BF16, offset=9472)


    L0 = W0 + 26624
    ln1f = A("ln1f", [128, 16384], BF16, offset=L0)
    ln2f = A("ln2f", [128, 16384], BF16, offset=L0)
    afm = A("afm", [128, 16384], BF16, offset=L0)
    ffb0 = A("ffb0", [128, 8192], BF16, offset=L0)
    ffb1 = A("ffb1", [128, 8192], BF16, offset=L0 + 16384)
    Q0 = L0 + 32768
    qt2 = A("qt2", [128, 32768], BF16, offset=Q0)
    y_s = A("y_s", [128, 16384], BF16, offset=Q0)
    wps = A("wps", [128, 8192], BF16, offset=Q0 + 32768)
    wpp = A("wpp", [128, 8192], BF16, offset=Q0 + 49152)
    ha_s = A("ha_s", [128, 32768], BF16, offset=Q0)
    K0 = Q0 + 65536
    kt2 = A("kt2", [128, 16384], BF16, offset=K0)
    wpstg = A("wpstg", [128, 8192], F32, offset=K0)
    hb_s = A("hb_s", [128, 32768], BF16, offset=K0)
    V0 = K0 + 32768
    vtk = A("vtk", [128, 8 * VST], BF16, offset=V0)
    dbgb = A("dbgb", [128, 2048], F32, offset=V0 + 35072)

    _sem_stack = ExitStack()
    sems = {e: _sem_stack.enter_context(nc.semaphore("sem_" + e)) for e in _ENGINES}

    def dma(dst, src, deps=(), ch="misc"):
        if ch not in sems:
            sems[ch] = _sem_stack.enter_context(nc.semaphore("sem_" + ch))
        return P.add("sync", lambda e, d=dst, s=src: e.dma_start(out=d, in_=s),
                     deps=deps, dma=True, key=ch)

    def vop(fn, deps=(), pdeps=(), small=False):
        return P.add("vector", fn, deps=deps, pdeps=pdeps, small=small)

    def sop(fn, deps=(), pdeps=()):
        return P.add("scalar", fn, deps=deps, pdeps=pdeps)

    def top(fn, deps=(), pdeps=(), grp=None, nopd=False):
        return P.add("tensor", fn, deps=deps, pdeps=pdeps, grp=grp, nopd=nopd)

    i_idn = dma(idn_s[:], idn_d)
    i_tri = dma(tri_s[:], tri_d)
    i_one = vop(lambda e: e.memset(one_s[:], 1.0))
    i_scl = dma(sclb[:], scl_d)
    i_lamdma = dma(rbuf[0:1, 0:1024].rearrange("p (a c) -> p a c", a=2), lam_d[0:2])
    i_lamdmb = dma(x2sc[0:1, 0:1024].rearrange("p (a c) -> p a c", a=2), lam_d[2:4])
    i_m1 = vop(lambda e: e.tensor_mul(outb2[0:1, 0:512], rbuf[0:1, 0:512],
                                      rbuf[0:1, 512:1024]), deps=[i_lamdma])
    i_m2 = vop(lambda e: e.tensor_mul(outb2[0:1, 512:1024], x2sc[0:1, 0:512],
                                      x2sc[0:1, 512:1024]), deps=[i_lamdmb])
    i_r1 = vop(lambda e: e.reduce_sum(smb[0:1, 20:21], outb2[0:1, 0:512], axis=AX),
               deps=[i_m1], small=True)
    i_r2 = vop(lambda e: e.reduce_sum(smb[0:1, 21:22], outb2[0:1, 512:1024], axis=AX),
               deps=[i_m2], small=True)
    i_e1 = sop(lambda e: e.activation(out=smb[0:1, 20:21], in_=smb[0:1, 20:21],
                                      func=AF.Exp), deps=[i_r1], pdeps=[i_r1])
    i_e2 = sop(lambda e: e.activation(out=smb[0:1, 21:22], in_=smb[0:1, 21:22],
                                      func=AF.Exp), deps=[i_r2], pdeps=[i_r2])
    i_ls = vop(lambda e: e.tensor_sub(smb[0:1, 22:23], smb[0:1, 20:21],
                                      smb[0:1, 21:22]), deps=[i_e1, i_e2],
               small=True)
    i_lv = vop(lambda e: e.tensor_scalar(smb[0:1, 22:23], smb[0:1, 22:23], 1.0, LI,
                                         ALU.mult, ALU.add), deps=[i_ls],
               small=True)
    i_lbf = vop(lambda e: e.tensor_copy(rb_bf[0:1, 0:1], smb[0:1, 22:23]),
                deps=[i_lv], small=True)
    i_eps = vop(lambda e: e.memset(smb[:, 30:31], EPS))

    psA = ExitStack()
    pqk0 = psA.enter_context(nc.psum_tensor("pqk0", [128, 512], F32))
    pqk1 = psA.enter_context(nc.psum_tensor("pqk1", [128, 512], F32))
    pat_a = psA.enter_context(nc.psum_tensor("pat_a", [128, 1024], BF16))
    pat_b = psA.enter_context(nc.psum_tensor("pat_b", [128, 1024], BF16))
    pav_a = psA.enter_context(nc.psum_tensor("pav_a", [128, 1024], BF16))
    pav_b = psA.enter_context(nc.psum_tensor("pav_b", [128, 1024], BF16))
    pnrm = psA.enter_context(nc.psum_tensor("pnrm", [128, 512], F32))

    i_lbc = top(lambda e: e.matmul(pnrm[:, 0:1], one_s[0:1, :], rb_bf[0:1, 0:1],
                                   start=True, stop=True), deps=[i_lbf, i_one])
    i_lamc = vop(lambda e: e.tensor_copy(lamc[:], pnrm[:, 0:1]), deps=[i_lbc],
                 small=True)

    i_nq0 = vop(lambda e: e.memset(nmq[:], 0.0))
    i_nk0 = vop(lambda e: e.memset(nmk[:], 0.0))

    def ln_stats(src_ap, rot, deps, sq_out):
        c = 8 * rot
        s1, s2 = smb[:, c:c + 1], smb[:, c + 1:c + 2]
        mu, var = smb[:, c + 2:c + 3], smb[:, c + 3:c + 4]
        rs, nb = smb[:, c + 4:c + 5], smb[:, c + 5:c + 6]
        mq = smb[:, c + 6:c + 7]
        i1 = vop(lambda e: e.reduce_sum(s1, src_ap, axis=AX), deps=deps,
                 small=True)
        i2 = sop(lambda e: e.activation(out=sq_out, in_=src_ap, func=AF.Square,
                                        accum_out=s2), deps=list(deps) + [i1])
        i3 = vop(lambda e: e.tensor_scalar(mu, s1, 1.0 / 1024.0, None, ALU.mult),
                 deps=[i1], pdeps=[i1], small=True)
        i4 = vop(lambda e: e.tensor_mul(mq, mu, mu), deps=[i3], small=True)
        i5 = vop(lambda e: e.scalar_tensor_tensor(var, s2, 1.0 / 1024.0, mq,
                                                  ALU.mult, ALU.subtract),
                 deps=[i2, i4], pdeps=[i2], small=True)
        i6 = sop(lambda e: e.activation(out=var, in_=var, func=AF.Ln,
                                        bias=smb[:, 30:31]),
                 deps=[i5], pdeps=[i5, i_eps])
        i7 = sop(lambda e: e.activation(out=rs, in_=var, func=AF.Exp, scale=-0.5),
                 deps=[i6])
        i8 = vop(lambda e: e.scalar_tensor_tensor(nb, mu, -1.0, rs, ALU.mult,
                                                  ALU.mult), deps=[i7], pdeps=[i7],
                 small=True)
        return i7, i8, rs, nb

    pat_banks = [pat_a, pat_b]
    pat_read = [None, None]
    pav_read = [None, None]
    gcnt = [0]

    xa_free = [None, None]
    l1_free = [None, None]
    for t in range(NT):
        rot = t % 2
        xslc = xt_a[:, 1024 * rot:1024 * (rot + 1)]
        i_x = dma(xslc, xt_v[t], deps=[xa_free[rot]], ch="x%d" % rot)
        i_rs, i_nb, rs, nb = ln_stats(xslc, rot, [i_x], x2sc[:])
        l1s = l1t_a[:, 1024 * rot:1024 * (rot + 1)]
        i_ap = sop(lambda e, s=xslc, d=l1s, r=rs, b=nb:
                   e.activation(out=d, in_=s, func=AF.Identity, scale=r, bias=b),
                   deps=[i_rs, i_nb, l1_free[rot]], pdeps=[i_rs, i_nb])
        xa_free[rot] = i_ap
        last_ev = None
        for k in range(8):
            bk = gcnt[0] % 2
            gcnt[0] += 1
            pb = pat_banks[bk]
            i_tr = top(lambda e, s=l1s, kk=k, d=pb[:, 0:128]:
                       e.transpose(d, s[:, 128 * kk:128 * (kk + 1)], idn_s[:]),
                       deps=[i_ap, i_idn, pat_read[bk]], grp=("t", 0, 0))
            i_ev = vop(lambda e, d=ln1f[:, 2048 * k + 128 * t:2048 * k + 128 * (t + 1)],
                       s=pb[:, 0:128]: e.tensor_copy(d, s), deps=[i_tr])
            pat_read[bk] = i_ev
            last_ev = i_ev
        l1_free[rot] = last_ev
    barA0 = P.barrier()

    if debug:
        dma(tap("ln1f", [128, 16384], BF16), ln1f[:], deps=[barA0["sync"]])

    tap_qkv = tap("qkv", [2048, 3072], BF16)
    tqv = tap_qkv.rearrange("(t p) c -> t p c", p=128) if debug else None

    i_wcast = [None, None]
    i_wdma_prev = None
    qkvb_free = [[], []]
    pqk_banks = [pqk0, pqk1]
    pqk_free = [None, None]
    nrm = smb[:, 24:28]
    nsc = smb[:, 28:29]
    nrm_prev = [None]
    q_ev_all = []
    k_ev_all = []
    v2_trs = {}
    step = [0]

    def norm_acc(qb, i_ev, acc, zdep):
        i_sq = sop(lambda e, s=qb: e.activation(out=sqsc[:], in_=s, func=AF.Square),
                   deps=[i_ev, nrm_prev[0]])
        i_rd = vop(lambda e: e.reduce_sum(
            nrm, sqsc[:].rearrange("p (j d) -> p j d", j=4), axis=AX), deps=[i_sq],
            small=True)
        i_mx = vop(lambda e: e.reduce_max(nsc, nrm, axis=AX), deps=[i_rd], small=True)
        i_ac = vop(lambda e, a=acc: e.tensor_max(a, a, nsc), deps=[i_mx, zdep],
                   small=True)
        nrm_prev[0] = i_ac
        return i_sq

    def extract_q(cs, tau, qb, i_ev):
        tq = tau % 8
        aa = tau // 8
        i_sq = norm_acc(qb, i_ev, nmq[:], i_nq0)
        bk = gcnt[0] % 2
        gcnt[0] += 1
        pb = pat_banks[bk]
        last = None
        for jj in range(4):
            last = top(lambda e, s=qb, j=jj, d=pb[0:64, 32 * jj:32 * jj + 32]:
                       e.transpose(d, s[0:32, 64 * j:64 * j + 64], idn_s[0:32, 0:32]),
                       deps=([i_ev, pat_read[bk]] if jj == 0 else []),
                       grp=("t", 0, 0))
        for jj in range(4):
            last = top(lambda e, s=qb, j=jj, d=pb[0:64, 128 + 32 * jj:160 + 32 * jj]:
                       e.transpose(d, s[64:96, 64 * j:64 * j + 64],
                                   idn_s[64:96, 64:96]),
                       deps=([i_ev] if jj == 0 else []), grp=("t", 64, 0))
        for jj in range(4):
            last = top(lambda e, s=qb, j=jj, d=pb[64:128, 256 + 64 * jj:320 + 64 * jj]:
                       e.transpose(d, s[0:64, 64 * j:64 * j + 64], idn_s[0:64, 0:64]),
                       deps=([i_ev] if jj == 0 else []), grp=("t", 0, 64))
        for jj in range(4):
            last = top(lambda e, s=qb, j=jj, d=pb[64:128, 512 + 64 * jj:576 + 64 * jj]:
                       e.transpose(d, s[64:128, 64 * j:64 * j + 64],
                                   idn_s[64:128, 64:128]),
                       deps=([i_ev] if jj == 0 else []), grp=("t", 64, 64))
        outs = []
        boff = 32 * aa + 4 * cs
        for pr in range(2):
            bo = (2 * tq + pr) * 2048 + boff
            qe = qt2[0:64, 0:1]
            de = _APc(qe.tensor, qe.offset + bo, [list(qe.ap[0]), [1, 4], [64, 32]])
            pe_ = pb[0:64, 0:1]
            se = _APc(pe_.tensor, pe_.offset + 128 * pr, [list(pe_.ap[0]), [32, 4], [1, 32]])
            i_e = vop(lambda e, s=se, d=de: e.tensor_copy(d, s), deps=[last])
            qo = qt2[64:128, 0:1]
            do2 = _APc(qo.tensor, qo.offset + bo, [list(qo.ap[0]), [1, 4], [64, 32]])
            po_ = pb[64:128, 0:1]
            so2 = _APc(po_.tensor, po_.offset + 256 + 256 * pr + 32,
                       [list(po_.ap[0]), [64, 4], [1, 32]])
            i_o = vop(lambda e, s=so2, d=do2: e.tensor_copy(d, s), deps=[last])
            pat_read[bk] = i_o
            outs.extend([i_e, i_o])
        q_ev_all.extend(outs)
        return outs + [i_sq]

    def extract_k(cs, tau, qb, i_ev):
        n = tau % 8
        aa = tau // 8
        i_sq = norm_acc(qb, i_ev, nmk[:], i_nk0)
        bk = gcnt[0] % 2
        gcnt[0] += 1
        pb = pat_banks[bk]
        last = None
        for jj in range(4):
            last = top(lambda e, s=qb, j=jj, d=pb[0:64, 128 * jj:128 * (jj + 1)]:
                       e.transpose(d, s[:, 64 * j:64 * j + 64], idn_s[:]),
                       deps=([i_ev, pat_read[bk]] if jj == 0 else []),
                       grp=("t", 0, 0))
        for jj in range(4):
            last = top(lambda e, s=qb, j=jj, d=pb[64:128, 128 * jj:128 * (jj + 1)]:
                       e.transpose(d, s[:, 64 * j:64 * j + 64], idn_s[:]),
                       deps=([i_ev] if jj == 0 else []), grp=("t", 0, 64))
        outs = []
        for jj in range(4):
            jp = 4 * (cs - 8) + jj
            dst = _mkap(kt2, n * 2048 + 8 * aa + jp, [[16, 128]])
            i_ev2 = vop(lambda e, s=pb[:, 128 * jj:128 * (jj + 1)], d=dst:
                        e.tensor_copy(d, s), deps=[last])
            outs.append(i_ev2)
        pat_read[bk] = outs[-1]
        k_ev_all.extend(outs)
        return outs + [i_sq]

    def extract_v1(csv, tau, qb, i_ev, vslot, slot_dep):
        aa = tau // 8
        outs = []
        for jj in range(4):
            jp = 4 * (csv - 10) + jj
            bk = gcnt[0] % 2
            gcnt[0] += 1
            pb = pat_banks[bk]
            i_t = top(lambda e, s=qb, j=jj, d=pb[64:128, 0:128]:
                      e.transpose(d, s[:, 64 * j:64 * j + 64], idn_s[:]),
                      deps=[i_ev, pat_read[bk]], grp=("t", 0, 64))
            vb = vth[64:128, 2048 * vslot:2048 * vslot + 1]
            dst = _APc(vb.tensor, vb.offset + 8 * aa + jp,
                       [list(vb.ap[0]), [16, 128]])
            i_ev2 = vop(lambda e, s=pb[64:128, 0:128], d=dst: e.tensor_copy(d, s),
                        deps=[i_t, slot_dep])
            pat_read[bk] = i_ev2
            outs.append(i_ev2)
        return outs

    def extract_v2(n, vslot, dep):
        outs = []
        for b in range(16):
            bk = b % 2
            pb = [pav_a, pav_b][bk]
            i_t = top(lambda e, s=vth[64:128, 2048 * vslot + 128 * b:
                                      2048 * vslot + 128 * (b + 1)],
                      d=pb[:, 0:64]: e.transpose(d, s, idn_s[64:128, 64:128]),
                      deps=[dep, pav_read[bk]], grp=("t", 64, 0))
            i_e = vop(lambda e, s=pb[:, 0:64],
                      d=vtk[:, n * VST + 128 * b + 64:n * VST + 128 * b + 128]:
                      e.tensor_copy(d, s), deps=[i_t, i_vones])
            pav_read[bk] = i_e
            v2_trs[n] = i_t
            outs.append(i_e)
        return outs

    i_vz = vop(lambda e: e.memset(vtk[:], 0.0))
    _ones_ap = _mkap(vtk, 0, [[VST, 8], [128, 17]])
    i_vones = vop(lambda e: e.memset(_ones_ap, 1.0), deps=[i_vz])

    def qkv_mm(wslc, tau, wdep, extra_mm_deps=()):
        par = step[0] % 2
        pq = pqk_banks[par]
        mm_last = None
        for k in range(8):
            mm_last = top(
                lambda e, o=pq[:, 0:256],
                l=ln1f[:, 2048 * k + 128 * tau:2048 * k + 128 * (tau + 1)],
                r=wslc[:, 256 * k:256 * (k + 1)], st=(k == 0), sp=(k == 7):
                e.matmul(o, l, r, start=st, stop=sp),
                deps=(([wdep, pqk_free[par], barA0["tensor"]] + list(extra_mm_deps))
                      if k == 0 else []))
        qsl = qkvb[:, 256 * par:256 * (par + 1)]
        i_ev = vop(lambda e, d=qsl, s=pq[:, 0:256]: e.tensor_copy(d, s),
                   deps=[mm_last] + qkvb_free[par])
        pqk_free[par] = i_ev
        step[0] += 1
        return qsl, i_ev, par

    for cs in range(10):
        slot = cs % 2
        wsl = wbf[:, 2048 * slot:2048 * (slot + 1)]
        d1 = dma(wstg[:].rearrange("p (k c) -> p k c", k=8), wqkv_d[cs],
                 deps=[i_wcast[1 - slot] if cs > 0 else barA0["sync"], i_wdma_prev],
                 ch="w")
        i_wdma_prev = d1
        c1 = vop(lambda e, d=wsl: e.tensor_copy(d, wstg[:]), deps=[d1])
        i_wcast[slot] = c1
        for tau in range(NT):
            qsl, i_ev, par = qkv_mm(wsl, tau, c1)
            i_tp = None
            if debug:
                i_tp = dma(tqv[tau][:, 256 * cs:256 * (cs + 1)], qsl, deps=[i_ev],
                           ch="tap")
            if cs < 8:
                readers = extract_q(cs, tau, qsl, i_ev)
            else:
                readers = extract_k(cs, tau, qsl, i_ev)
            qkvb_free[par] = readers[-2:] + ([i_tp] if i_tp else [])

    d10 = dma(wstg[:].rearrange("p (k c) -> p k c", k=8), wqkv_d[10],
              deps=[i_wcast[1], i_wdma_prev], ch="w")
    c10 = vop(lambda e: e.tensor_copy(wbf[:, 0:2048], wstg[:]), deps=[d10, i_wcast[0]])
    d11 = dma(wstg[:].rearrange("p (k c) -> p k c", k=8), wqkv_d[11], deps=[c10],
              ch="w")
    c11 = vop(lambda e: e.tensor_copy(wbf[:, 2048:4096], wstg[:]), deps=[d11])
    wc = {10: c10, 11: c11}
    v1_done = {}
    v2_done = {}
    for tau in [x for n_ in range(8) for x in (n_, n_ + 8)]:
        n = tau % 8
        aa = tau // 8
        vslot = n % 2
        slot_dep = v2_trs.get(n - 2)
        for csv in (10, 11):
            wsl = wbf[:, 2048 * (csv - 10):2048 * (csv - 9)]
            qsl, i_ev, par = qkv_mm(wsl, tau, wc[csv])
            i_tp = None
            if debug:
                i_tp = dma(tqv[tau][:, 256 * csv:256 * (csv + 1)], qsl, deps=[i_ev],
                           ch="tap")
            outs = extract_v1(csv, tau, qsl, i_ev, vslot, slot_dep)
            v1_done.setdefault(n, []).extend(outs)
            qkvb_free[par] = outs[-1:] + ([i_tp] if i_tp else [])
        if aa == 1:
            v2_done[n] = extract_v2(n, vslot, v1_done[n][-1])
    v2_all = [it for n_ in range(8) for it in v2_done[n_]]

    i_cq = vop(lambda e: e.tensor_copy(rb_bf[:, 2:3], nmq[:]), deps=[nrm_prev[0]],
               small=True)
    i_ck = vop(lambda e: e.tensor_copy(rb_bf[:, 3:4], nmk[:]), deps=[nrm_prev[0]],
               small=True)
    i_tq = top(lambda e: e.transpose(pat_a[0:1, 0:128], rb_bf[:, 2:3], idn_s[:]),
               deps=[i_cq, pat_read[0]], grp=("t", 0, 0))
    i_tk = top(lambda e: e.transpose(pat_b[0:1, 0:128], rb_bf[:, 3:4], idn_s[:]),
               deps=[i_ck, pat_read[1]], grp=("t", 0, 0))
    i_qm = vop(lambda e: e.reduce_max(smb[0:1, 16:17], pat_a[0:1, 0:128], axis=AX),
               deps=[i_tq], small=True)
    i_km = vop(lambda e: e.reduce_max(smb[0:1, 17:18], pat_b[0:1, 0:128], axis=AX),
               deps=[i_tk], small=True)
    i_qk = vop(lambda e: e.tensor_mul(smb[0:1, 18:19], smb[0:1, 16:17],
                                      smb[0:1, 17:18]), deps=[i_qm, i_km],
               small=True)
    i_s40 = vop(lambda e: e.tensor_copy(smb[0:1, 40:41], smb[0:1, 18:19]),
                deps=[i_qk])
    i_s41 = vop(lambda e: e.tensor_copy(smb[0:1, 41:42], smb[0:1, 16:17]),
                deps=[i_qk])
    i_lnm = sop(lambda e: e.activation(out=smb[0:1, 18:19], in_=smb[0:1, 18:19],
                                       func=AF.Ln), deps=[i_qk, i_s40],
                pdeps=[i_qk, i_s40])
    i_s42 = sop(lambda e: e.copy(smb[0:1, 42:43], smb[0:1, 18:19]), deps=[i_lnm])
    i_exm = sop(lambda e: e.activation(out=smb[0:1, 18:19], in_=smb[0:1, 18:19],
                                       func=AF.Exp, scale=0.5), deps=[i_lnm, i_s42])
    i_bs = vop(lambda e: e.tensor_scalar(smb[0:1, 19:20], smb[0:1, 18:19],
                                         -1.03 / 8.0, -0.1, ALU.mult, ALU.add),
               deps=[i_exm], pdeps=[i_exm], small=True)
    i_bbf = vop(lambda e: e.tensor_copy(rb_bf[0:1, 4:5], smb[0:1, 19:20]),
                deps=[i_bs], small=True)
    i_bbc = top(lambda e: e.matmul(pnrm[:, 1:2], one_s[0:1, :], rb_bf[0:1, 4:5],
                                   start=True, stop=True), deps=[i_bbf, i_lamc])
    i_bsm = vop(lambda e: e.tensor_copy(bsm[:], pnrm[:, 1:2]), deps=[i_bbc],
                small=True)

    if debug:
        dma(tap("qt2", [128, 32768], F32) if False else tap("qt2b", [128, 32768], BF16),
            qt2[:], deps=q_ev_all[-1:], ch="tap")
        dma(tap("kt2b", [128, 16384], BF16), kt2[:], deps=k_ev_all[-1:], ch="tap")
        dma(tap("vtkb", [128, 8 * VST], BF16), vtk[:], deps=v2_all[-1:], ch="tap")
        dma(tap("bsm", [128, 1]), bsm[:], deps=[i_bsm], ch="tap")
        dma(tap("smb", [128, 64]), smb[:], deps=[i_bsm], ch="tap")
        dma(tap("nmq", [128, 1]), nmq[:], deps=[i_bsm], ch="tap")
        dma(tap("nmk", [128, 1]), nmk[:], deps=[i_bsm], ch="tap")
        dma(tap("lamc", [128, 1]), lamc[:], deps=[i_lamc], ch="tap")

    barA = P.barrier()
    psA.close()

    if "B" not in phases:
        P.add("sync", lambda e: e.nop(), deps=[barA["sync"]])
        return _finish(nc, P, sems, taps)

    psB = ExitStack()
    pS = [psB.enter_context(nc.psum_tensor("ps0", [128, 1024], F32)),
          psB.enter_context(nc.psum_tensor("ps1", [128, 1024], F32))]
    pC = psB.enter_context(nc.psum_tensor("pc", [128, 1024], F32))
    pU0 = psB.enter_context(nc.psum_tensor("pu0", [128, 512], F32))
    pU1 = psB.enter_context(nc.psum_tensor("pu1", [128, 512], F32))

    iz0 = vop(lambda e: e.memset(pS[0][:], 0.0), deps=[barA["vector"]])
    iz1 = vop(lambda e: e.memset(pS[1][:], 0.0), deps=[iz0])
    pS_read = [iz1, iz1]
    pt_read = [None, None]
    pU_read = [None, None]
    ub_free = [[None, None], [None, None]]
    pC_read = [None]

    GB = ("B",)

    def emit_smm(hp, c, b, nb_):
        n = hp // 2
        sb = max(0, 128 * (b - 4 * c))
        par = b % 2
        sreg = pS[par]
        i_s0 = top(lambda e, o=sreg[:, sb:512],
                   l=kt2[0:64, n * 2048 + 128 * b:n * 2048 + 128 * (b + 1)],
                   r=qt2[0:64, hp * 2048 + 512 * c + sb:
                         hp * 2048 + 512 * (c + 1)]:
                   e.matmul(o, l, r, start=True, stop=True,
                            tile_position=(0, 0)),
                   deps=[pS_read[par], barA["tensor"]], grp=GB)
        i_s1 = top(lambda e, o=sreg[:, 512 + sb:1024],
                   l=kt2[64:128, n * 2048 + 128 * b:n * 2048 + 128 * (b + 1)],
                   r=qt2[64:128, hp * 2048 + 512 * c + sb:
                         hp * 2048 + 512 * (c + 1)]:
                   e.matmul(o, l, r, start=True, stop=True,
                            tile_position=(64, 0)), deps=[], grp=GB, nopd=True)
        ptsl = pt_s[:, 1024 * par:1024 * (par + 1)]
        i_ex = sop(lambda e, o=ptsl, s=sreg:
                   e.activation(out=o, in_=s[:], func=AF.Exp, scale=0.125,
                                bias=bsm[:, 0:1]),
                   deps=[i_s1, pt_read[par], i_bsm], pdeps=[i_bsm])
        pS_read[par] = i_ex
        dep_av = i_ex
        if b >= 4 * c:
            ll = 128 * (b - 4 * c)
            i_mk0 = vop(lambda e, o=ptsl[:, ll:ll + 128]:
                        e.tensor_mul(o, o, tri_s[:]), deps=[i_ex, i_tri])
            i_mk1 = vop(lambda e, o=ptsl[:, 512 + ll:512 + ll + 128]:
                        e.tensor_mul(o, o, tri_s[:]), deps=[i_mk0])
            dep_av = i_mk1
        return par, ptsl, dep_av

    def emit_av(hp, c, b, nb_, par, ptsl, dep_av):
        n = hp // 2
        vcol = 64 if hp % 2 == 0 else 0
        sb = max(0, 128 * (b - 4 * c))
        vap = vtk[:, n * VST + 128 * b + vcol:n * VST + 128 * b + vcol + 128]
        i_a0 = top(lambda e, o=pU0[:, sb:512], l=vap, r=ptsl[:, sb:512],
                   st=(b == 0), sp=(b == nb_ - 1):
                   e.matmul(o, l, r, start=st, stop=sp),
                   deps=[dep_av] + ([pU_read[0]] if b == 0 else []), grp=GB)
        i_a1 = top(lambda e, o=pU1[:, sb:512], l=vap,
                   r=ptsl[:, 512 + sb:1024],
                   st=(b == 0), sp=(b == nb_ - 1):
                   e.matmul(o, l, r, start=st, stop=sp),
                   deps=([pU_read[1]] if b == 0 else []), grp=GB)
        pt_read[par] = i_a1
        return i_a0, i_a1

    def emit_head(hp, c, av0_last, av1_last):
        l_row = 64 if hp % 2 == 0 else 0
        tpar = (hp * 4 + c) % 2
        u0b, u1b = ub0s[tpar], ub1s[tpar]
        i_u0 = vop(lambda e, d=u0b: e.tensor_copy(d[:], pU0[:]),
                   deps=[av0_last, ub_free[tpar][0]])
        i_u1 = vop(lambda e, d=u1b: e.tensor_copy(d[:], pU1[:]),
                   deps=[av1_last, ub_free[tpar][1]])
        pU_read[0] = i_u0
        pU_read[1] = i_u1
        i_r0 = vop(lambda e, lr=l_row, s=u0b: e.reciprocal(
            rbuf[lr:lr + 1, 0:512], s[lr:lr + 1, :]),
                   deps=[i_u0], small=True)
        i_r1 = vop(lambda e, lr=l_row, s=u1b: e.reciprocal(
            rbuf[lr:lr + 1, 512:1024], s[lr:lr + 1, :]),
                   deps=[i_u1], small=True)
        i_c1 = sop(lambda e, lr=l_row: e.mul(rbuf[lr:lr + 1, 512:1024],
                                             rbuf[lr:lr + 1, 512:1024],
                                             lamc[lr:lr + 1, 0:1]),
                   deps=[i_r1], pdeps=[i_r1, i_lamc])
        i_snA = None
        if debug and hp == 2 and c == 3:
            i_snA = vop(lambda e, lr=l_row: e.tensor_copy(
                dbgb[lr:lr + 1, 0:1024], rbuf[lr:lr + 1, 0:1024]),
                deps=[i_r0, i_c1])
        i_cb = vop(lambda e, lr=l_row: e.tensor_copy(rb_bf[lr:lr + 1, 0:1024],
                                                     rbuf[lr:lr + 1, 0:1024]),
                   deps=[i_r0, i_c1, i_snA])
        return i_u0, i_u1, i_cb

    def emit_rest(hp, c, i_u0, i_u1, i_cb):
        n = hp // 2
        pr_lo = 0 if hp % 2 == 0 else 64
        l_row = 64 if hp % 2 == 0 else 0
        tpar = (hp * 4 + c) % 2
        u0b, u1b = ub0s[tpar], ub1s[tpar]
        i_b0 = top(lambda e, lr=l_row, pl=pr_lo: e.matmul(
            pC[pl:pl + 64, 0:512], one_s[lr:lr + 1, 0:64],
            rb_bf[lr:lr + 1, 0:512], start=True, stop=True),
            deps=[i_cb, pC_read[0]], grp=GB)
        i_b1 = top(lambda e, lr=l_row, pl=pr_lo: e.matmul(
            pC[pl:pl + 64, 512:1024], one_s[lr:lr + 1, 64:128],
            rb_bf[lr:lr + 1, 512:1024], start=True, stop=True), deps=[],
            grp=GB)
        i_ce = vop(lambda e, pl=pr_lo: e.tensor_copy(rbuf[pl:pl + 64, 0:1024],
                                                     pC[pl:pl + 64, 0:1024]),
                   deps=[i_b1])
        pC_read[0] = i_ce
        i_snB = None
        if debug and hp == 2 and c == 3:
            i_snB = vop(lambda e, pl=pr_lo: e.tensor_copy(
                dbgb[pl:pl + 64, 1024:2048], rbuf[pl:pl + 64, 0:1024]),
                deps=[i_ce])
            i_snC = vop(lambda e, s=u0b: e.tensor_copy(dbgb[64:128, 1024:1536],
                                                       s[64:128, :]), deps=[i_ce])
            dma(tap("dbgb", [128, 2048]), dbgb[:], deps=[i_snB, i_snC], ch="tap")
        i_t0 = vop(lambda e, pl=pr_lo, s=u0b: e.tensor_mul(
            t0_s[pl:pl + 64, :], s[pl:pl + 64, :], rbuf[pl:pl + 64, 0:512]),
                   deps=[i_ce, i_snB, i_u0])
        i_t1 = vop(lambda e, pl=pr_lo, s=u1b: e.tensor_mul(
            t1_s[pl:pl + 64, :], s[pl:pl + 64, :], rbuf[pl:pl + 64, 512:1024]),
                   deps=[i_ce, i_u1])
        i_af = vop(lambda e, pl=pr_lo,
                   d=afm[pr_lo:pr_lo + 64,
                         n * 2048 + 512 * c:n * 2048 + 512 * (c + 1)]:
                   e.tensor_sub(d, t0_s[pl:pl + 64, :], t1_s[pl:pl + 64, :]),
                   deps=[i_t0, i_t1])
        ub_free[tpar] = [i_t0, i_t1]

    steps = [(hp_, c_, b_, 4 * c_ + 4)
             for hp_ in range(16) for c_ in range(4) for b_ in range(4 * c_ + 4)]
    pend = None
    pend_rest = [None]

    def close_combine(php, pc, a0, a1):
        if pend_rest[0] is not None:
            emit_rest(*pend_rest[0])
        hd = emit_head(php, pc, a0, a1)
        pend_rest[0] = (php, pc) + hd

    for stp in steps:
        smm = emit_smm(*stp)
        if pend is not None:
            (php, pc, pb, pnb), psmm = pend
            a0, a1 = emit_av(php, pc, pb, pnb, *psmm)
            if pb == pnb - 1:
                close_combine(php, pc, a0, a1)
        pend = (stp, smm)
    (php, pc, pb, pnb), psmm = pend
    a0, a1 = emit_av(php, pc, pb, pnb, *psmm)
    close_combine(php, pc, a0, a1)
    emit_rest(*pend_rest[0])

    barB = P.barrier()
    psB.close()
    if debug:
        dma(tap("afmb", [128, 16384], BF16), afm[:], deps=[barB["sync"]], ch="tap")

    if "C" not in phases:
        P.add("sync", lambda e: e.nop(), deps=[P.tails["sync"]])
        return _finish(nc, P, sems, taps)

    psC = ExitStack()
    pY = [psC.enter_context(nc.psum_tensor("py0", [128, 1024], F32)),
          psC.enter_context(nc.psum_tensor("py1", [128, 1024], F32))]
    pCT = psC.enter_context(nc.psum_tensor("pct", [128, 2048], BF16))

    i_wp = dma(wpstg[:].rearrange("p (k c) -> p k c", k=8), wproj_d,
               deps=[barB["sync"]], ch="w")
    i_wps = vop(lambda e: e.tensor_scalar(wps[:], wpstg[:], ONE_MINUS_LI, None,
                                          ALU.mult), deps=[i_wp, barB["vector"]])
    i_wpp = vop(lambda e: e.tensor_copy(wpp[:], wpstg[:]), deps=[i_wp])

    pY_read = [barB["vector"], barB["vector"]]
    y_ev = []
    for tau in range(NT):
        par = tau % 2
        mm_last = None
        for oc in range(2):
            for k in range(8):
                mm_last = top(
                    lambda e, o=pY[par][:, 512 * oc:512 * (oc + 1)],
                    l=afm[:, 2048 * k + 128 * tau:2048 * k + 128 * (tau + 1)],
                    r=wps[:, 1024 * k + 512 * oc:1024 * k + 512 * (oc + 1)],
                    st=(k == 0), sp=(k == 7):
                    e.matmul(o, l, r, start=st, stop=sp),
                    deps=([i_wps, pY_read[par], barB["tensor"]] if k == 0 else []))
        i_ye = vop(lambda e, d=y_s[:, 1024 * tau:1024 * (tau + 1)], s=pY[par][:]:
                   e.tensor_copy(d, s), deps=[mm_last])
        pY_read[par] = i_ye
        y_ev.append(i_ye)
    if debug:
        dma(tap("yb", [128, 16384], BF16), y_s[:], deps=[y_ev[-1]], ch="tap")

    if "C" in phases and "c1" in phases:
        P.add("sync", lambda e: e.nop(), deps=[P.tails["sync"]])
        return _finish(nc, P, sems, taps)

    xc_free = [None, None]
    x1_free = [None, None]
    pct_read = [None, None]
    ln2_ev = []
    x1dma = []
    x1d_prev = [None, None]
    zsc_free = [None]
    for st in range(NT):
        p_, ut = st // 8, st % 8
        par = st % 2
        mm_last = None
        for oc2 in range(2):
            for kk in range(8):
                mm_last = top(
                    lambda e, o=pY[par][:, 512 * oc2:512 * (oc2 + 1)],
                    l=y_s[:, 1024 * (8 * p_ + kk) + 128 * ut:
                          1024 * (8 * p_ + kk) + 128 * (ut + 1)],
                    r=wpp[:, 1024 * kk + 512 * oc2:1024 * kk + 512 * (oc2 + 1)],
                    st_=(kk == 0), sp=(kk == 7):
                    e.matmul(o, l, r, start=st_, stop=sp),
                    deps=([i_wpp, pY_read[par], y_ev[-1]] if kk == 0 else []))
        rot = st % 2
        i_rs, i_nb, rs, nb = ln_stats(pY[par][:], rot, [mm_last], x2sc[:])
        i_z = sop(lambda e, s=pY[par][:], r=rs, b=nb:
                  e.activation(out=zsc_c[:], in_=s, func=AF.Identity, scale=r,
                               bias=b),
                  deps=[i_rs, i_nb, zsc_free[0]], pdeps=[i_rs, i_nb])
        i_x = dma(xt_c[:, 1024 * rot:1024 * (rot + 1)], xt_v[st],
                  deps=[xc_free[rot], barB["sync"]], ch="x%d" % rot)
        x1sl = x1t_c[:, 1024 * rot:1024 * (rot + 1)]
        i_x1 = vop(lambda e, d=x1sl, xs=xt_c[:, 1024 * rot:1024 * (rot + 1)]:
                   e.tensor_add(d, zsc_c[:], xs),
                   deps=[i_z, i_x, x1_free[rot], x1d_prev[rot]])
        zsc_free[0] = i_x1
        pY_read[par] = i_x1
        xc_free[rot] = i_x1
        i_xd = dma(x1_v[st], x1sl, deps=[i_x1], ch="xo%d" % rot)
        x1dma.append(i_xd)
        x1d_prev[rot] = i_xd
        if "nl2" in phases:
            x1_free[rot] = i_xd
            ln2_ev.append(i_x1)
            continue
        i_rs2, i_nb2, rs2, nb2 = ln_stats(x1sl, rot, [i_x1], x2sc[:])
        i_l2 = sop(lambda e, s=x1sl, r=rs2, b=nb2:
                   e.activation(out=l2t_c[:], in_=s, func=AF.Identity,
                                scale=r, bias=b),
                   deps=[i_rs2, i_nb2, ln2_ev[-1] if ln2_ev else None],
                   pdeps=[i_rs2, i_nb2])
        half = pCT[:, 1024 * (st % 2):1024 * (st % 2 + 1)]
        trs = []
        for k in range(8):
            trs.append(top(lambda e, s=l2t_c[:, 128 * k:128 * (k + 1)],
                           d=half[:, 128 * k:128 * (k + 1)]:
                           e.transpose(d, s, idn_s[:]),
                           deps=([i_l2, pct_read[st % 2]] if k == 0 else []),
                           grp=("t", 0, 0)))
        dst = _mkap(ln2f, 8192 * (st // 8) + 128 * (st % 8), [[1024, 8], [1, 128]])
        srcp = half.rearrange("p (k c) -> p k c", k=8)
        i_l2e = vop(lambda e, d=dst, s=srcp: e.tensor_copy(d, s), deps=[trs[-1]])
        pct_read[st % 2] = i_l2e
        x1_free[rot] = i_l2e
        ln2_ev.append(i_l2e)

    barC = P.barrier()
    psC.close()
    if debug:
        dma(tap("ln2fb", [128, 16384], BF16), ln2f[:], deps=[barC["sync"]], ch="tap")
        dma(tap("x1", [2048, 1024]), x1_d, deps=[barC["sync"]], ch="tap")

    if "D" not in phases:
        P.add("sync", lambda e: e.nop(), deps=[P.tails["sync"]])
        return _finish(nc, P, sems, taps)

    psD = ExitStack()
    pDa = psD.enter_context(nc.psum_tensor("pda", [128, 1024], F32))
    pDb = psD.enter_context(nc.psum_tensor("pdb", [128, 1024], F32))
    pDc = psD.enter_context(nc.psum_tensor("pdc", [128, 1024], F32))
    pDT = psD.enter_context(nc.psum_tensor("pdt", [128, 2048], BF16))

    dstg_free = [None, None, None, None]
    dbf_free = [None, None, None, None]
    slab_i = [0]

    def stream_slab(src_ap, deps=()):
        si = slab_i[0]
        slab_i[0] += 1
        sl = si % 4
        st_ = dstg[:, 1024 * sl:1024 * (sl + 1)]
        bf_ = dbf[:, 1024 * sl:1024 * (sl + 1)]
        i_d = dma(st_.rearrange("p (k c) -> p k c", k=8), src_ap,
                  deps=list(deps) + [dstg_free[sl]], ch="fw%d" % sl)
        i_c = vop(lambda e, d=bf_, s=st_: e.tensor_copy(d, s),
                  deps=[i_d, dbf_free[sl]])
        dstg_free[sl] = i_c
        return bf_, i_c, sl

    scales = {"w1": sclb[:, 0:32], "w2": sclb[:, 32:64], "h1": sclb[:, 64:96],
              "h2": sclb[:, 96:128], "w3": sclb[:, 128:136]}
    ffbs = [ffb0, ffb1]
    out_done = []
    prev_w3_mm = [barC["tensor"]]
    prev_h2_mm = [barC["tensor"]]
    pDT_read = [None, None]
    for tt in range(2):
        l2base = 8192 * tt
        pDa_read = [barC["vector"], barC["vector"]]
        pDb_read = [barC["vector"], barC["vector"]]
        gu_free = [None, None, None, None]
        h0_last = None
        for m in range(32):
            b1, c1_, sl1 = stream_slab(w1_d[m, 0],
                                       deps=[barC["sync"]] if (tt == 0 and m < 2)
                                       else ())
            b2, c2_, sl2 = stream_slab(w2_d[m, 0],
                                       deps=[barC["sync"]] if (tt == 0 and m < 2)
                                       else ())
            mg_last = mu_last = None
            for tc in range(2):
                pi = tc
                mg = mu_ = None
                for k in range(8):
                    mg = top(lambda e, o=pDa[:, 512 * pi:512 * (pi + 1)],
                             l=b1[:, 128 * k:128 * (k + 1)],
                             r=ln2f[:, l2base + 1024 * k + 512 * tc:
                                    l2base + 1024 * k + 512 * (tc + 1)],
                             st=(k == 0), sp=(k == 7):
                             e.matmul(o, l, r, start=st, stop=sp),
                             deps=([c1_, pDa_read[pi], barC["tensor"]]
                                   if k == 0 else []))
                for k in range(8):
                    mu_ = top(lambda e, o=pDb[:, 512 * pi:512 * (pi + 1)],
                              l=b2[:, 128 * k:128 * (k + 1)],
                              r=ln2f[:, l2base + 1024 * k + 512 * tc:
                                     l2base + 1024 * k + 512 * (tc + 1)],
                              st=(k == 0), sp=(k == 7):
                              e.matmul(o, l, r, start=st, stop=sp),
                              deps=([c2_, pDb_read[pi]] if k == 0 else []))
                gq = pt_s[:, 512 * pi:512 * (pi + 1)]
                uq = pt_s[:, 1024 + 512 * pi:1024 + 512 * (pi + 1)]
                i_g = sop(lambda e, o=gq, s=pDa[:, 512 * pi:512 * (pi + 1)],
                          sc=scales["w1"][:, m:m + 1]:
                          e.activation(out=o, in_=s, func=AF.Silu, scale=sc),
                          deps=[mg, gu_free[pi], i_scl], pdeps=[i_scl])
                i_u = sop(lambda e, o=uq, s=pDb[:, 512 * pi:512 * (pi + 1)],
                          sc=scales["w2"][:, m:m + 1]:
                          e.activation(out=o, in_=s, func=AF.Identity, scale=sc),
                          deps=[mu_, gu_free[2 + pi]], pdeps=[i_scl])
                pDa_read[pi] = i_g
                pDb_read[pi] = i_u
                i_h0 = vop(lambda e, d=ha_s[:, 1024 * m + 512 * tc:
                                            1024 * m + 512 * (tc + 1)],
                           g=gq, u=uq: e.tensor_mul(d, g, u),
                           deps=[i_g, i_u] + ([prev_w3_mm[0]]
                                              if (m == 0 and tc == 0) else []))
                gu_free[pi] = i_h0
                gu_free[2 + pi] = i_h0
                h0_last = i_h0
                mg_last, mu_last = mg, mu_
            dbf_free[sl1] = mg_last
            dbf_free[sl2] = mu_last

        def big_layer(w_dram, scl_ap, src, dst, func, nm, nk, dep0, dst_free):
            evs = []
            pr_read = [dep0, dep0]
            last_mm = [None]
            for m in range(nm):
                slabs = []
                for q in range(nk // 8):
                    slabs.append(stream_slab(w_dram[m, q]))
                mmf = None
                for tc in range(2):
                    pi = tc
                    mm = None
                    for k in range(nk):
                        bq, cq, _ = slabs[k // 8]
                        mm = top(lambda e, o=pDc[:, 512 * pi:512 * (pi + 1)],
                                 l=bq[:, 128 * (k % 8):128 * (k % 8 + 1)],
                                 r=src[:, 1024 * k + 512 * tc:
                                       1024 * k + 512 * (tc + 1)],
                                 st=(k == 0), sp=(k == nk - 1):
                                 e.matmul(o, l, r, start=st, stop=sp),
                                 deps=(([cq, pr_read[pi], dep0] if k == 0 else
                                        ([cq] if k % 8 == 0 else []))))
                    i_o = sop(lambda e, o=dst[:, 1024 * m + 512 * tc:
                                              1024 * m + 512 * (tc + 1)],
                              s=pDc[:, 512 * pi:512 * (pi + 1)],
                              sc=scl_ap[:, m:m + 1], f=func:
                              e.activation(out=o, in_=s, func=f, scale=sc),
                              deps=[mm, dst_free], pdeps=[i_scl])
                    pr_read[pi] = i_o
                    evs.append(i_o)
                    mmf = mm
                for _, cq, slq in slabs:
                    dbf_free[slq] = mmf
                last_mm[0] = mmf
            return evs, last_mm[0]

        if debug and tt == 0:
            dma(tap("h0", [128, 32768], BF16), ha_s[:], deps=[h0_last], ch="tap")
        h1_ev, h1_mm = big_layer(h1_d, scales["h1"], ha_s, hb_s, AF.Gelu, 32, 32,
                                 h0_last, prev_h2_mm[0])
        if debug and tt == 0:
            dma(tap("h1", [128, 32768], BF16), hb_s[:], deps=[h1_ev[-1]], ch="tap")
        h2_ev, h2_mm = big_layer(h2_d, scales["h2"], hb_s, ha_s, AF.Gelu, 32, 32,
                                 h1_ev[-1], None)
        prev_h2_mm[0] = h2_mm
        if debug and tt == 0:
            dma(tap("h2", [128, 32768], BF16), ha_s[:], deps=[h2_ev[-1]], ch="tap")
        f_ev, w3_mm = big_layer(w3_d, scales["w3"], ha_s, ffbs[tt], AF.Identity,
                                8, 32, h2_ev[-1], None)
        if debug and tt == 0:
            dma(tap("ff", [128, 8192], BF16), ffbs[tt][:], deps=[f_ev[-1]], ch="tap")
        prev_w3_mm[0] = w3_mm

        xd_free = [None, None]
        ob_free = [None, None]
        x2_free = [None]
        outbs = [outb_a, outb2]
        for ss in range(8):
            st = 8 * tt + ss
            rot = ss % 2
            half = pDT[:, 1024 * rot:1024 * (rot + 1)]
            trs = []
            for k in range(8):
                trs.append(top(lambda e, s=ffbs[tt][:, 1024 * k + 128 * ss:
                                                    1024 * k + 128 * (ss + 1)],
                               d=half[:, 128 * k:128 * (k + 1)]:
                               e.transpose(d, s, idn_s[:]),
                               deps=([f_ev[-1], pDT_read[rot]] if k == 0 else []),
                               grp=("t", 0, 0)))
            i_rs, i_nb, rs, nb = ln_stats(half[:], rot, [trs[-1]], sqbf[:])
            i_z = sop(lambda e, s=half[:], r=rs, b=nb:
                      e.activation(out=x2sc[:], in_=s, func=AF.Identity,
                                   scale=r, bias=b),
                      deps=[i_rs, i_nb, x2_free[0]], pdeps=[i_rs, i_nb])
            xts = [xt_d0, xt_d1][rot]
            i_x = dma(xts[:], x1_v[st],
                      deps=[x1dma[st], xd_free[rot], barC["sync"]], ch="x%d" % rot)
            i_o = vop(lambda e, d=outbs[rot][:], xs=xts[:]:
                      e.tensor_add(d, x2sc[:], xs),
                      deps=[i_z, i_x, ob_free[rot]])
            x2_free[0] = i_o
            pDT_read[rot] = i_o
            xd_free[rot] = i_o
            i_od = dma(out_v[st], outbs[rot][:], deps=[i_o], ch="out%d" % rot)
            ob_free[rot] = i_od
            out_done.append(i_od)

    psD.close()
    P.add("sync", lambda e: e.nop(), deps=out_done)
    return _finish(nc, P, sems, taps)


def _finish(nc, P, sems, taps):
    P.assign()
    hwm = {}
    with nc.Block() as blk:
        @blk.sync
        def _(eng):
            P.emit_engine("sync", eng, sems, hwm)

        @blk.tensor
        def _(eng):
            P.emit_engine("tensor", eng, sems, hwm)

        @blk.vector
        def _(eng):
            P.emit_engine("vector", eng, sems, hwm)

        @blk.scalar
        def _(eng):
            P.emit_engine("scalar", eng, sems, hwm)

        @blk.gpsimd
        def _(eng):
            P.emit_engine("gpsimd", eng, sems, hwm)

    return nc, taps


def _host_layout(inputs):
    f32 = np.float32
    Wqkv = np.ascontiguousarray(np.asarray(inputs["L0_attn_Wqkv"], f32))
    wqkv_t = np.ascontiguousarray(
        Wqkv.reshape(8, 128, 12, 256).transpose(2, 1, 0, 3))
    wproj_t = np.ascontiguousarray(
        np.asarray(inputs["L0_attn_Wproj"], f32).reshape(8, 128, 1024)
        .transpose(1, 0, 2))
    lamv = np.ascontiguousarray(np.stack([
        np.asarray(inputs["L0_attn_lambda_q1"], f32),
        np.asarray(inputs["L0_attn_lambda_k1"], f32),
        np.asarray(inputs["L0_attn_lambda_q2"], f32),
        np.asarray(inputs["L0_attn_lambda_k2"], f32)], axis=0))

    def wtile(w, nm, nk):
        wt = np.asarray(w, f32).T
        t = wt.reshape(nk // 8, 8, 128, nm, 128).transpose(3, 0, 2, 1, 4)
        return np.ascontiguousarray(t)

    w1t = wtile(inputs["L0_ffn_w1_w"], 32, 8)
    w2t = wtile(inputs["L0_ffn_w2_w"], 32, 8)
    h1t = wtile(inputs["L0_ffn_h1_w"], 32, 32)
    h2t = wtile(inputs["L0_ffn_h2_w"], 32, 32)
    w3t = wtile(inputs["L0_ffn_w3_w"], 8, 32)
    scl = np.zeros((128, 136), f32)
    scl[:, 0:32] = np.asarray(inputs["L0_ffn_w1_s"], f32).reshape(32, 128).T
    scl[:, 32:64] = np.asarray(inputs["L0_ffn_w2_s"], f32).reshape(32, 128).T
    scl[:, 64:96] = np.asarray(inputs["L0_ffn_h1_s"], f32).reshape(32, 128).T
    scl[:, 96:128] = np.asarray(inputs["L0_ffn_h2_s"], f32).reshape(32, 128).T
    scl[:, 128:136] = np.asarray(inputs["L0_ffn_w3_s"], f32).reshape(8, 128).T
    return dict(wqkv=wqkv_t, wproj=wproj_t, lamv=lamv, w1t=w1t, w2t=w2t,
                h1t=h1t, h2t=h2t, w3t=w3t, scl=scl)


def kernel(**inputs):
    shared = _host_layout(inputs)
    x = np.asarray(inputs["L0_attn_x"], np.float32)
    nc, _ = _build(debug=False)
    in_maps = []
    for b in range(8):
        xu = np.ascontiguousarray(
            np.concatenate([x[b, 0::2], x[b, 1::2]], axis=0))
        m = {"x": xu}
        m.update(shared)
        in_maps.append(m)
    res = run_bass_kernel_spmd(nc, in_maps, core_ids=list(range(8)))
    out = np.empty((8, 2048, 1024), np.float32)
    for b in range(8):
        o = res.results[b]["out"]
        out[b, 0::2] = o[:1024]
        out[b, 1::2] = o[1024:]
    return out


# version 05 (final)
# speedup vs baseline: 1.2894x; 1.0470x over previous
import math
import numpy as np
import ml_dtypes
import concourse.bass as bass
import concourse.mybir as mybir
from concourse.ap import AP as _APc
from concourse.bass_utils import run_bass_kernel_spmd
from contextlib import ExitStack

F32 = mybir.dt.float32
BF16 = mybir.dt.bfloat16
AX = mybir.AxisListType.X
AF = mybir.ActivationFunctionType
ALU = mybir.AluOpType
NPBF = ml_dtypes.bfloat16

LI = 0.8 - 0.6 * math.exp(-0.3)
ONE_MINUS_LI = 1.0 - LI
NT = 16
EPS = 1e-5
VST = 2112

_ENGINES = ("sync", "tensor", "vector", "scalar", "gpsimd")


class _Item:
    __slots__ = ("eng", "fn", "deps", "pdeps", "amt", "need", "post", "key",
                 "small")


class _Prog:
    def __init__(self):
        self.items = []
        self.tails = {e: None for e in _ENGINES}

    def add(self, eng, fn, deps=(), pdeps=(), dma=False, grp=None, key=None,
            nopd=False, small=False):
        it = _Item()
        it.eng = eng
        it.fn = fn
        it.small = small
        it.key = key if key is not None else eng
        it.deps = [d for d in deps if d is not None]
        it.pdeps = [d for d in pdeps if d is not None]
        it.amt = 16 if dma else 1
        it.need = False
        it.post = 0
        if dma:
            prev = getattr(self, "_dma_last", {}).get(it.key)
            if prev is not None:
                it.pdeps.append(prev)
            self.__dict__.setdefault("_dma_last", {})[it.key] = it
        if eng == "tensor":
            lg = getattr(self, "_tgrp", None)
            if (grp is not None or lg is not None) and grp != lg                     and self.tails["tensor"] is not None:
                it.pdeps.append(self.tails["tensor"])
            self._tgrp = grp
        self.items.append(it)
        self.tails[eng] = it
        return it

    def barrier(self):
        tails = [self.tails[e] for e in _ENGINES if self.tails[e] is not None]
        out = {}
        for e in _ENGINES:
            out[e] = self.add(e, lambda eng: eng.nop(), deps=list(tails))
        return out

    def assign(self):
        for it in self.items:
            if it.amt == 16:
                it.need = True
            for d in it.deps:
                if d.eng != it.eng or d.key != it.key or \
                        (it.eng == "vector" and d.small):
                    d.need = True
            for d in it.pdeps:
                d.need = True
        cnt = {}
        for it in self.items:
            if it.need:
                cnt[it.key] = cnt.get(it.key, 0) + it.amt
            it.post = cnt.get(it.key, 0)

    def emit_engine(self, ename, eng, sems, hwm):
        h = hwm.setdefault(ename, {})
        for it in self.items:
            if it.eng != ename:
                continue
            waits = {}
            for d in it.deps:
                if (d.eng != ename or d.key != it.key
                        or (ename == "vector" and d.small)) \
                        and d.post > waits.get(d.key, 0):
                    waits[d.key] = d.post
            for d in it.pdeps:
                if d.post > waits.get(d.key, 0):
                    waits[d.key] = d.post
            for se, v in waits.items():
                if v > h.get(se, 0):
                    eng.wait_ge(sems[se], v)
                    h[se] = v
            inst = it.fn(eng)
            if it.need:
                inst.then_inc(sems[it.key], it.amt)


def _mkap(t_ap, off_elems, dims):
    b = t_ap[:, 0:1]
    return _APc(b.tensor, b.offset + off_elems, [list(b.ap[0])] + [list(d) for d in dims])


def _build(debug=False, phases="ABCD"):
    nc = bass.Bass("TRN2", target_bir_lowering=False,
                   detect_race_conditions=False, dynamic_dma_scratch_size=512)
    P = _Prog()

    x_d = nc.dram_tensor("x", [2048, 1024], F32, kind="ExternalInput").ap()
    wqkv_d = nc.dram_tensor("wqkv", [12, 128, 8, 256], F32, kind="ExternalInput").ap()
    wproj_d = nc.dram_tensor("wproj", [128, 8, 1024], F32, kind="ExternalInput").ap()
    lam_d = nc.dram_tensor("lamv", [4, 512], F32, kind="ExternalInput").ap()
    w1_d = nc.dram_tensor("w1t", [32, 1, 128, 8, 128], F32, kind="ExternalInput").ap()
    w2_d = nc.dram_tensor("w2t", [32, 1, 128, 8, 128], F32, kind="ExternalInput").ap()
    h1_d = nc.dram_tensor("h1t", [32, 4, 128, 8, 128], F32, kind="ExternalInput").ap()
    h2_d = nc.dram_tensor("h2t", [32, 4, 128, 8, 128], F32, kind="ExternalInput").ap()
    w3_d = nc.dram_tensor("w3t", [8, 4, 128, 8, 128], F32, kind="ExternalInput").ap()
    scl_d = nc.dram_tensor("scl", [128, 136], F32, kind="ExternalInput").ap()
    out_d = nc.dram_tensor("out", [2048, 1024], F32, kind="ExternalOutput").ap()
    x1_d = nc.dram_tensor("x1buf", [2048, 1024], F32, kind="Internal").ap()

    idn_d = nc.inline_tensor(np.eye(128, dtype=NPBF), "idn").ap()
    tri_np = np.asarray(np.arange(128)[None, :] >= np.arange(128)[:, None], dtype=NPBF)
    tri_d = nc.inline_tensor(tri_np, "tri").ap()

    taps = {}

    def tap(name, shape, dtype=F32):
        if not debug:
            return None
        taps[name] = nc.dram_tensor("tap_" + name, shape, dtype,
                                    kind="ExternalOutput").ap()
        return taps[name]

    xt_v = x_d.rearrange("(t p) c -> t p c", p=128)
    out_v = out_d.rearrange("(t p) c -> t p c", p=128)
    x1_v = x1_d.rearrange("(t p) c -> t p c", p=128)

    SB0 = 768

    def A(name, shape, dtype, offset):
        return nc.alloc_sbuf_tensor_at(name, shape, dtype, offset=offset + SB0)
    idn_s = A("idn_s", [128, 128], BF16, offset=0)
    tri_s = A("tri_s", [128, 128], BF16, offset=256)
    one_s = A("one_s", [128, 128], BF16, offset=512)
    lamc = A("lamc", [128, 1], F32, offset=768)
    bsm = A("bsm", [128, 1], F32, offset=800)
    nmq = A("nmq", [128, 1], F32, offset=832)
    nmk = A("nmk", [128, 1], F32, offset=864)
    smb = A("smb", [128, 64], F32, offset=896)
    pt_s = A("pt", [128, 2048], BF16, offset=1280)
    rbuf = A("rbuf", [128, 1024], F32, offset=5376)
    rb_bf = A("rb_bf", [128, 1024], BF16, offset=9472)
    x2sc = A("x2sc", [128, 1024], F32, offset=11520)
    t0_s = A("t0", [128, 512], F32, offset=11520)
    t1_s = A("t1", [128, 512], F32, offset=13568)
    outb2 = A("outb2", [128, 1024], F32, offset=15616)
    xt_d0 = A("xt_d0", [128, 1024], F32, offset=221504)
    xt_d1 = A("xt_d1", [128, 1024], F32, offset=216832)
    outb_a = A("outb_a", [128, 1024], F32, offset=5376)

    W0 = 19968
    wstg = A("wstg", [128, 2048], F32, offset=W0)
    wbf = A("wbf", [128, 4096], BF16, offset=W0 + 8192)
    qkvb = A("qkvb", [128, 512], BF16, offset=W0 + 16384)
    sqsc = A("sqsc", [128, 256], F32, offset=W0 + 17408)
    vth = A("vth", [128, 4096], BF16, offset=W0 + 18432)
    xt_a = A("xt_a", [128, 2048], F32, offset=W0)
    l1t_a = A("l1t_a", [128, 2048], BF16, offset=W0 + 8192)
    xt_c = A("xt_c", [128, 2048], F32, offset=W0)
    zsc_c = A("zsc_c", [128, 1024], F32, offset=W0 + 8192)
    x1t_c = A("x1t_c", [128, 2048], F32, offset=W0 + 12288)
    l2t_c = A("l2t_c", [128, 1024], BF16, offset=W0 + 20480)
    ub0s = [A("ub0a", [128, 512], F32, offset=212736),
            A("ub0b", [128, 512], F32, offset=W0)]
    ub1s = [A("ub1a", [128, 512], F32, offset=214784),
            A("ub1b", [128, 512], F32, offset=W0 + 2048)]
    dstg = A("dstg", [128, 4096], F32, offset=W0)
    dbf = A("dbf", [128, 4096], BF16, offset=W0 + 16384)
    sclb = A("sclb", [128, 136], F32, offset=220928)
    sqbf = A("sqbf", [128, 1024], BF16, offset=9472)


    L0 = W0 + 26624
    ln1f = A("ln1f", [128, 16384], BF16, offset=L0)
    ln2f = A("ln2f", [128, 16384], BF16, offset=L0)
    afm = A("afm", [128, 16384], BF16, offset=L0)
    ffb0 = A("ffb0", [128, 8192], BF16, offset=L0)
    ffb1 = A("ffb1", [128, 8192], BF16, offset=L0 + 16384)
    Q0 = L0 + 32768
    qt2 = A("qt2", [128, 32768], BF16, offset=Q0)
    y_s = A("y_s", [128, 16384], BF16, offset=Q0)
    wps = A("wps", [128, 8192], BF16, offset=Q0 + 32768)
    wpp = A("wpp", [128, 8192], BF16, offset=Q0 + 49152)
    ha_s = A("ha_s", [128, 32768], BF16, offset=Q0)
    K0 = Q0 + 65536
    kt2 = A("kt2", [128, 16384], BF16, offset=K0)
    wpstg = A("wpstg", [128, 8192], F32, offset=K0)
    hb_s = A("hb_s", [128, 32768], BF16, offset=K0)
    V0 = K0 + 32768
    vtk = A("vtk", [128, 8 * VST], BF16, offset=V0)
    dbgb = A("dbgb", [128, 2048], F32, offset=V0 + 35072)

    _sem_stack = ExitStack()
    sems = {e: _sem_stack.enter_context(nc.semaphore("sem_" + e)) for e in _ENGINES}

    def dma(dst, src, deps=(), ch="misc"):
        if ch not in sems:
            sems[ch] = _sem_stack.enter_context(nc.semaphore("sem_" + ch))
        return P.add("sync", lambda e, d=dst, s=src: e.dma_start(out=d, in_=s),
                     deps=deps, dma=True, key=ch)

    def vop(fn, deps=(), pdeps=(), small=False):
        return P.add("vector", fn, deps=deps, pdeps=pdeps, small=small)

    def sop(fn, deps=(), pdeps=()):
        return P.add("scalar", fn, deps=deps, pdeps=pdeps)

    def top(fn, deps=(), pdeps=(), grp=None, nopd=False):
        return P.add("tensor", fn, deps=deps, pdeps=pdeps, grp=grp, nopd=nopd)

    i_idn = dma(idn_s[:], idn_d)
    i_tri = dma(tri_s[:], tri_d)
    i_one = vop(lambda e: e.memset(one_s[:], 1.0))
    i_scl = dma(sclb[:], scl_d)
    i_lamdma = dma(rbuf[0:1, 0:1024].rearrange("p (a c) -> p a c", a=2), lam_d[0:2])
    i_lamdmb = dma(x2sc[0:1, 0:1024].rearrange("p (a c) -> p a c", a=2), lam_d[2:4])
    i_m1 = vop(lambda e: e.tensor_mul(outb2[0:1, 0:512], rbuf[0:1, 0:512],
                                      rbuf[0:1, 512:1024]), deps=[i_lamdma])
    i_m2 = vop(lambda e: e.tensor_mul(outb2[0:1, 512:1024], x2sc[0:1, 0:512],
                                      x2sc[0:1, 512:1024]), deps=[i_lamdmb])
    i_r1 = vop(lambda e: e.reduce_sum(smb[0:1, 20:21], outb2[0:1, 0:512], axis=AX),
               deps=[i_m1], small=True)
    i_r2 = vop(lambda e: e.reduce_sum(smb[0:1, 21:22], outb2[0:1, 512:1024], axis=AX),
               deps=[i_m2], small=True)
    i_e1 = sop(lambda e: e.activation(out=smb[0:1, 20:21], in_=smb[0:1, 20:21],
                                      func=AF.Exp), deps=[i_r1], pdeps=[i_r1])
    i_e2 = sop(lambda e: e.activation(out=smb[0:1, 21:22], in_=smb[0:1, 21:22],
                                      func=AF.Exp), deps=[i_r2], pdeps=[i_r2])
    i_ls = vop(lambda e: e.tensor_sub(smb[0:1, 22:23], smb[0:1, 20:21],
                                      smb[0:1, 21:22]), deps=[i_e1, i_e2],
               small=True)
    i_lv = vop(lambda e: e.tensor_scalar(smb[0:1, 22:23], smb[0:1, 22:23], 1.0, LI,
                                         ALU.mult, ALU.add), deps=[i_ls],
               small=True)
    i_lbf = vop(lambda e: e.tensor_copy(rb_bf[0:1, 0:1], smb[0:1, 22:23]),
                deps=[i_lv], small=True)
    i_eps = vop(lambda e: e.memset(smb[:, 30:31], EPS))

    psA = ExitStack()
    pqk0 = psA.enter_context(nc.psum_tensor("pqk0", [128, 512], F32))
    pqk1 = psA.enter_context(nc.psum_tensor("pqk1", [128, 512], F32))
    pat_a = psA.enter_context(nc.psum_tensor("pat_a", [128, 1024], BF16))
    pat_b = psA.enter_context(nc.psum_tensor("pat_b", [128, 1024], BF16))
    pav_a = psA.enter_context(nc.psum_tensor("pav_a", [128, 1024], BF16))
    pav_b = psA.enter_context(nc.psum_tensor("pav_b", [128, 1024], BF16))
    pnrm = psA.enter_context(nc.psum_tensor("pnrm", [128, 512], F32))

    i_lbc = top(lambda e: e.matmul(pnrm[:, 0:1], one_s[0:1, :], rb_bf[0:1, 0:1],
                                   start=True, stop=True), deps=[i_lbf, i_one])
    i_lamc = vop(lambda e: e.tensor_copy(lamc[:], pnrm[:, 0:1]), deps=[i_lbc],
                 small=True)

    i_nq0 = vop(lambda e: e.memset(nmq[:], 0.0))
    i_nk0 = vop(lambda e: e.memset(nmk[:], 0.0))

    def ln_stats(src_ap, rot, deps, sq_out):
        c = 8 * rot
        s1, s2 = smb[:, c:c + 1], smb[:, c + 1:c + 2]
        mu, var = smb[:, c + 2:c + 3], smb[:, c + 3:c + 4]
        rs, nb = smb[:, c + 4:c + 5], smb[:, c + 5:c + 6]
        mq = smb[:, c + 6:c + 7]
        i1 = vop(lambda e: e.reduce_sum(s1, src_ap, axis=AX), deps=deps,
                 small=True)
        i2 = sop(lambda e: e.activation(out=sq_out, in_=src_ap, func=AF.Square,
                                        accum_out=s2), deps=list(deps) + [i1])
        i3 = vop(lambda e: e.tensor_scalar(mu, s1, 1.0 / 1024.0, None, ALU.mult),
                 deps=[i1], pdeps=[i1], small=True)
        i4 = vop(lambda e: e.tensor_mul(mq, mu, mu), deps=[i3], small=True)
        i5 = vop(lambda e: e.scalar_tensor_tensor(var, s2, 1.0 / 1024.0, mq,
                                                  ALU.mult, ALU.subtract),
                 deps=[i2, i4], pdeps=[i2], small=True)
        i6 = sop(lambda e: e.activation(out=var, in_=var, func=AF.Ln,
                                        bias=smb[:, 30:31]),
                 deps=[i5], pdeps=[i5, i_eps])
        i7 = sop(lambda e: e.activation(out=rs, in_=var, func=AF.Exp, scale=-0.5),
                 deps=[i6])
        i8 = vop(lambda e: e.scalar_tensor_tensor(nb, mu, -1.0, rs, ALU.mult,
                                                  ALU.mult), deps=[i7], pdeps=[i7],
                 small=True)
        return i7, i8, rs, nb

    pat_banks = [pat_a, pat_b]
    pat_read = [None, None]
    pav_read = [None, None]
    gcnt = [0]

    xa_free = [None, None]
    l1_free = [None, None]
    for t in range(NT):
        rot = t % 2
        xslc = xt_a[:, 1024 * rot:1024 * (rot + 1)]
        i_x = dma(xslc, xt_v[t], deps=[xa_free[rot]], ch="x%d" % rot)
        i_rs, i_nb, rs, nb = ln_stats(xslc, rot, [i_x], x2sc[:])
        l1s = l1t_a[:, 1024 * rot:1024 * (rot + 1)]
        i_ap = sop(lambda e, s=xslc, d=l1s, r=rs, b=nb:
                   e.activation(out=d, in_=s, func=AF.Identity, scale=r, bias=b),
                   deps=[i_rs, i_nb, l1_free[rot]], pdeps=[i_rs, i_nb])
        xa_free[rot] = i_ap
        last_ev = None
        for k in range(8):
            bk = gcnt[0] % 2
            gcnt[0] += 1
            pb = pat_banks[bk]
            i_tr = top(lambda e, s=l1s, kk=k, d=pb[:, 0:128]:
                       e.transpose(d, s[:, 128 * kk:128 * (kk + 1)], idn_s[:]),
                       deps=[i_ap, i_idn, pat_read[bk]], grp=("t", 0, 0))
            i_ev = vop(lambda e, d=ln1f[:, 2048 * k + 128 * t:2048 * k + 128 * (t + 1)],
                       s=pb[:, 0:128]: e.tensor_copy(d, s), deps=[i_tr])
            pat_read[bk] = i_ev
            last_ev = i_ev
        l1_free[rot] = last_ev
    barA0 = P.barrier()

    if debug:
        dma(tap("ln1f", [128, 16384], BF16), ln1f[:], deps=[barA0["sync"]])

    tap_qkv = tap("qkv", [2048, 3072], BF16)
    tqv = tap_qkv.rearrange("(t p) c -> t p c", p=128) if debug else None

    i_wcast = [None, None]
    i_wdma_prev = None
    qkvb_free = [[], []]
    pqk_banks = [pqk0, pqk1]
    pqk_free = [None, None]
    nrm = smb[:, 24:28]
    nsc = smb[:, 28:29]
    nrm_prev = [None]
    q_ev_all = []
    k_ev_all = []
    v2_trs = {}
    step = [0]

    def norm_acc(qb, i_ev, acc, zdep):
        i_sq = sop(lambda e, s=qb: e.activation(out=sqsc[:], in_=s, func=AF.Square),
                   deps=[i_ev, nrm_prev[0]])
        i_rd = vop(lambda e: e.reduce_sum(
            nrm, sqsc[:].rearrange("p (j d) -> p j d", j=4), axis=AX), deps=[i_sq],
            small=True)
        i_mx = vop(lambda e: e.reduce_max(nsc, nrm, axis=AX), deps=[i_rd], small=True)
        i_ac = vop(lambda e, a=acc: e.tensor_max(a, a, nsc), deps=[i_mx, zdep],
                   small=True)
        nrm_prev[0] = i_ac
        return i_sq

    def extract_q(cs, tau, qb, i_ev):
        tq = tau % 8
        aa = tau // 8
        i_sq = norm_acc(qb, i_ev, nmq[:], i_nq0)
        bk = gcnt[0] % 2
        gcnt[0] += 1
        pb = pat_banks[bk]
        last = None
        for jj in range(4):
            last = top(lambda e, s=qb, j=jj, d=pb[0:64, 32 * jj:32 * jj + 32]:
                       e.transpose(d, s[0:32, 64 * j:64 * j + 64], idn_s[0:32, 0:32]),
                       deps=([i_ev, pat_read[bk]] if jj == 0 else []),
                       grp=("t", 0, 0))
        for jj in range(4):
            last = top(lambda e, s=qb, j=jj, d=pb[0:64, 128 + 32 * jj:160 + 32 * jj]:
                       e.transpose(d, s[64:96, 64 * j:64 * j + 64],
                                   idn_s[64:96, 64:96]),
                       deps=([i_ev] if jj == 0 else []), grp=("t", 64, 0))
        for jj in range(4):
            last = top(lambda e, s=qb, j=jj, d=pb[64:128, 256 + 64 * jj:320 + 64 * jj]:
                       e.transpose(d, s[0:64, 64 * j:64 * j + 64], idn_s[0:64, 0:64]),
                       deps=([i_ev] if jj == 0 else []), grp=("t", 0, 64))
        for jj in range(4):
            last = top(lambda e, s=qb, j=jj, d=pb[64:128, 512 + 64 * jj:576 + 64 * jj]:
                       e.transpose(d, s[64:128, 64 * j:64 * j + 64],
                                   idn_s[64:128, 64:128]),
                       deps=([i_ev] if jj == 0 else []), grp=("t", 64, 64))
        outs = []
        boff = 32 * aa + 4 * cs
        for pr in range(2):
            bo = (2 * tq + pr) * 2048 + boff
            qe = qt2[0:64, 0:1]
            de = _APc(qe.tensor, qe.offset + bo, [list(qe.ap[0]), [1, 4], [64, 32]])
            pe_ = pb[0:64, 0:1]
            se = _APc(pe_.tensor, pe_.offset + 128 * pr, [list(pe_.ap[0]), [32, 4], [1, 32]])
            i_e = vop(lambda e, s=se, d=de: e.tensor_copy(d, s), deps=[last])
            qo = qt2[64:128, 0:1]
            do2 = _APc(qo.tensor, qo.offset + bo, [list(qo.ap[0]), [1, 4], [64, 32]])
            po_ = pb[64:128, 0:1]
            so2 = _APc(po_.tensor, po_.offset + 256 + 256 * pr + 32,
                       [list(po_.ap[0]), [64, 4], [1, 32]])
            i_o = vop(lambda e, s=so2, d=do2: e.tensor_copy(d, s), deps=[last])
            pat_read[bk] = i_o
            outs.extend([i_e, i_o])
        q_ev_all.extend(outs)
        return outs + [i_sq]

    def extract_k(cs, tau, qb, i_ev):
        n = tau % 8
        aa = tau // 8
        i_sq = norm_acc(qb, i_ev, nmk[:], i_nk0)
        bk = gcnt[0] % 2
        gcnt[0] += 1
        pb = pat_banks[bk]
        last = None
        for jj in range(4):
            last = top(lambda e, s=qb, j=jj, d=pb[0:64, 128 * jj:128 * (jj + 1)]:
                       e.transpose(d, s[:, 64 * j:64 * j + 64], idn_s[:]),
                       deps=([i_ev, pat_read[bk]] if jj == 0 else []),
                       grp=("t", 0, 0))
        for jj in range(4):
            last = top(lambda e, s=qb, j=jj, d=pb[64:128, 128 * jj:128 * (jj + 1)]:
                       e.transpose(d, s[:, 64 * j:64 * j + 64], idn_s[:]),
                       deps=([i_ev] if jj == 0 else []), grp=("t", 0, 64))
        outs = []
        for jj in range(4):
            jp = 4 * (cs - 8) + jj
            dst = _mkap(kt2, n * 2048 + 8 * aa + jp, [[16, 128]])
            i_ev2 = vop(lambda e, s=pb[:, 128 * jj:128 * (jj + 1)], d=dst:
                        e.tensor_copy(d, s), deps=[last])
            outs.append(i_ev2)
        pat_read[bk] = outs[-1]
        k_ev_all.extend(outs)
        return outs + [i_sq]

    def extract_v1(csv, tau, qb, i_ev, vslot, slot_dep):
        aa = tau // 8
        outs = []
        for jj in range(4):
            jp = 4 * (csv - 10) + jj
            bk = gcnt[0] % 2
            gcnt[0] += 1
            pb = pat_banks[bk]
            i_t = top(lambda e, s=qb, j=jj, d=pb[64:128, 0:128]:
                      e.transpose(d, s[:, 64 * j:64 * j + 64], idn_s[:]),
                      deps=[i_ev, pat_read[bk]], grp=("t", 0, 64))
            vb = vth[64:128, 2048 * vslot:2048 * vslot + 1]
            dst = _APc(vb.tensor, vb.offset + 8 * aa + jp,
                       [list(vb.ap[0]), [16, 128]])
            i_ev2 = vop(lambda e, s=pb[64:128, 0:128], d=dst: e.tensor_copy(d, s),
                        deps=[i_t, slot_dep])
            pat_read[bk] = i_ev2
            outs.append(i_ev2)
        return outs

    def extract_v2(n, vslot, dep):
        outs = []
        for b in range(16):
            bk = b % 2
            pb = [pav_a, pav_b][bk]
            i_t = top(lambda e, s=vth[64:128, 2048 * vslot + 128 * b:
                                      2048 * vslot + 128 * (b + 1)],
                      d=pb[:, 0:64]: e.transpose(d, s, idn_s[64:128, 64:128]),
                      deps=[dep, pav_read[bk]], grp=("t", 64, 0))
            i_e = vop(lambda e, s=pb[:, 0:64],
                      d=vtk[:, n * VST + 128 * b + 64:n * VST + 128 * b + 128]:
                      e.tensor_copy(d, s), deps=[i_t, i_vones])
            pav_read[bk] = i_e
            v2_trs[n] = i_t
            outs.append(i_e)
        return outs

    i_vz = vop(lambda e: e.memset(vtk[:], 0.0))
    _ones_ap = _mkap(vtk, 0, [[VST, 8], [128, 17]])
    i_vones = vop(lambda e: e.memset(_ones_ap, 1.0), deps=[i_vz])

    def qkv_mm(wslc, tau, wdep, extra_mm_deps=()):
        par = step[0] % 2
        pq = pqk_banks[par]
        mm_last = None
        for k in range(8):
            mm_last = top(
                lambda e, o=pq[:, 0:256],
                l=ln1f[:, 2048 * k + 128 * tau:2048 * k + 128 * (tau + 1)],
                r=wslc[:, 256 * k:256 * (k + 1)], st=(k == 0), sp=(k == 7):
                e.matmul(o, l, r, start=st, stop=sp),
                deps=(([wdep, pqk_free[par], barA0["tensor"]] + list(extra_mm_deps))
                      if k == 0 else []))
        qsl = qkvb[:, 256 * par:256 * (par + 1)]
        i_ev = vop(lambda e, d=qsl, s=pq[:, 0:256]: e.tensor_copy(d, s),
                   deps=[mm_last] + qkvb_free[par])
        pqk_free[par] = i_ev
        step[0] += 1
        return qsl, i_ev, par

    for cs in range(10):
        slot = cs % 2
        wsl = wbf[:, 2048 * slot:2048 * (slot + 1)]
        d1 = dma(wstg[:].rearrange("p (k c) -> p k c", k=8), wqkv_d[cs],
                 deps=[i_wcast[1 - slot] if cs > 0 else barA0["sync"], i_wdma_prev],
                 ch="w")
        i_wdma_prev = d1
        c1 = vop(lambda e, d=wsl: e.tensor_copy(d, wstg[:]), deps=[d1])
        i_wcast[slot] = c1
        for tau in range(NT):
            qsl, i_ev, par = qkv_mm(wsl, tau, c1)
            i_tp = None
            if debug:
                i_tp = dma(tqv[tau][:, 256 * cs:256 * (cs + 1)], qsl, deps=[i_ev],
                           ch="tap")
            if cs < 8:
                readers = extract_q(cs, tau, qsl, i_ev)
            else:
                readers = extract_k(cs, tau, qsl, i_ev)
            qkvb_free[par] = readers[-2:] + ([i_tp] if i_tp else [])

    d10 = dma(wstg[:].rearrange("p (k c) -> p k c", k=8), wqkv_d[10],
              deps=[i_wcast[1], i_wdma_prev], ch="w")
    c10 = vop(lambda e: e.tensor_copy(wbf[:, 0:2048], wstg[:]), deps=[d10, i_wcast[0]])
    d11 = dma(wstg[:].rearrange("p (k c) -> p k c", k=8), wqkv_d[11], deps=[c10],
              ch="w")
    c11 = vop(lambda e: e.tensor_copy(wbf[:, 2048:4096], wstg[:]), deps=[d11])
    wc = {10: c10, 11: c11}
    v1_done = {}
    v2_done = {}
    for tau in [x for n_ in range(8) for x in (n_, n_ + 8)]:
        n = tau % 8
        aa = tau // 8
        vslot = n % 2
        slot_dep = v2_trs.get(n - 2)
        for csv in (10, 11):
            wsl = wbf[:, 2048 * (csv - 10):2048 * (csv - 9)]
            qsl, i_ev, par = qkv_mm(wsl, tau, wc[csv])
            i_tp = None
            if debug:
                i_tp = dma(tqv[tau][:, 256 * csv:256 * (csv + 1)], qsl, deps=[i_ev],
                           ch="tap")
            outs = extract_v1(csv, tau, qsl, i_ev, vslot, slot_dep)
            v1_done.setdefault(n, []).extend(outs)
            qkvb_free[par] = outs[-1:] + ([i_tp] if i_tp else [])
        if aa == 1:
            v2_done[n] = extract_v2(n, vslot, v1_done[n][-1])
    v2_all = [it for n_ in range(8) for it in v2_done[n_]]

    i_cq = vop(lambda e: e.tensor_copy(rb_bf[:, 2:3], nmq[:]), deps=[nrm_prev[0]],
               small=True)
    i_ck = vop(lambda e: e.tensor_copy(rb_bf[:, 3:4], nmk[:]), deps=[nrm_prev[0]],
               small=True)
    i_tq = top(lambda e: e.transpose(pat_a[0:1, 0:128], rb_bf[:, 2:3], idn_s[:]),
               deps=[i_cq, pat_read[0]], grp=("t", 0, 0))
    i_tk = top(lambda e: e.transpose(pat_b[0:1, 0:128], rb_bf[:, 3:4], idn_s[:]),
               deps=[i_ck, pat_read[1]], grp=("t", 0, 0))
    i_qm = vop(lambda e: e.reduce_max(smb[0:1, 16:17], pat_a[0:1, 0:128], axis=AX),
               deps=[i_tq], small=True)
    i_km = vop(lambda e: e.reduce_max(smb[0:1, 17:18], pat_b[0:1, 0:128], axis=AX),
               deps=[i_tk], small=True)
    i_qk = vop(lambda e: e.tensor_mul(smb[0:1, 18:19], smb[0:1, 16:17],
                                      smb[0:1, 17:18]), deps=[i_qm, i_km],
               small=True)
    i_s40 = vop(lambda e: e.tensor_copy(smb[0:1, 40:41], smb[0:1, 18:19]),
                deps=[i_qk])
    i_s41 = vop(lambda e: e.tensor_copy(smb[0:1, 41:42], smb[0:1, 16:17]),
                deps=[i_qk])
    i_lnm = sop(lambda e: e.activation(out=smb[0:1, 18:19], in_=smb[0:1, 18:19],
                                       func=AF.Ln), deps=[i_qk, i_s40],
                pdeps=[i_qk, i_s40])
    i_s42 = sop(lambda e: e.copy(smb[0:1, 42:43], smb[0:1, 18:19]), deps=[i_lnm])
    i_exm = sop(lambda e: e.activation(out=smb[0:1, 18:19], in_=smb[0:1, 18:19],
                                       func=AF.Exp, scale=0.5), deps=[i_lnm, i_s42])
    i_bs = vop(lambda e: e.tensor_scalar(smb[0:1, 19:20], smb[0:1, 18:19],
                                         -1.03 / 8.0, -0.1, ALU.mult, ALU.add),
               deps=[i_exm], pdeps=[i_exm], small=True)
    i_bbf = vop(lambda e: e.tensor_copy(rb_bf[0:1, 4:5], smb[0:1, 19:20]),
                deps=[i_bs], small=True)
    i_bbc = top(lambda e: e.matmul(pnrm[:, 1:2], one_s[0:1, :], rb_bf[0:1, 4:5],
                                   start=True, stop=True), deps=[i_bbf, i_lamc])
    i_bsm = vop(lambda e: e.tensor_copy(bsm[:], pnrm[:, 1:2]), deps=[i_bbc],
                small=True)

    if debug:
        dma(tap("qt2", [128, 32768], F32) if False else tap("qt2b", [128, 32768], BF16),
            qt2[:], deps=q_ev_all[-1:], ch="tap")
        dma(tap("kt2b", [128, 16384], BF16), kt2[:], deps=k_ev_all[-1:], ch="tap")
        dma(tap("vtkb", [128, 8 * VST], BF16), vtk[:], deps=v2_all[-1:], ch="tap")
        dma(tap("bsm", [128, 1]), bsm[:], deps=[i_bsm], ch="tap")
        dma(tap("smb", [128, 64]), smb[:], deps=[i_bsm], ch="tap")
        dma(tap("nmq", [128, 1]), nmq[:], deps=[i_bsm], ch="tap")
        dma(tap("nmk", [128, 1]), nmk[:], deps=[i_bsm], ch="tap")
        dma(tap("lamc", [128, 1]), lamc[:], deps=[i_lamc], ch="tap")

    barA = P.barrier()
    psA.close()

    if "B" not in phases:
        P.add("sync", lambda e: e.nop(), deps=[barA["sync"]])
        return _finish(nc, P, sems, taps)

    psB = ExitStack()
    pS = [psB.enter_context(nc.psum_tensor("ps0", [128, 1024], F32)),
          psB.enter_context(nc.psum_tensor("ps1", [128, 1024], F32))]
    pC = psB.enter_context(nc.psum_tensor("pc", [128, 1024], F32))
    pU0 = psB.enter_context(nc.psum_tensor("pu0", [128, 512], F32))
    pU1 = psB.enter_context(nc.psum_tensor("pu1", [128, 512], F32))

    iz0 = vop(lambda e: e.memset(pS[0][:], 0.0), deps=[barA["vector"]])
    iz1 = vop(lambda e: e.memset(pS[1][:], 0.0), deps=[iz0])
    pS_read = [iz1, iz1]
    pt_read = [None, None]
    pU_read = [None, None]
    ub_free = [[None, None], [None, None]]
    pC_read = [None]

    GB = ("B",)

    def emit_smm(hp, c, b, nb_):
        n = hp // 2
        sb = max(0, 128 * (b - 4 * c))
        par = b % 2
        sreg = pS[par]
        i_s0 = top(lambda e, o=sreg[:, sb:512],
                   l=kt2[0:64, n * 2048 + 128 * b:n * 2048 + 128 * (b + 1)],
                   r=qt2[0:64, hp * 2048 + 512 * c + sb:
                         hp * 2048 + 512 * (c + 1)]:
                   e.matmul(o, l, r, start=True, stop=True,
                            tile_position=(0, 0)),
                   deps=[pS_read[par], barA["tensor"]], grp=GB)
        i_s1 = top(lambda e, o=sreg[:, 512 + sb:1024],
                   l=kt2[64:128, n * 2048 + 128 * b:n * 2048 + 128 * (b + 1)],
                   r=qt2[64:128, hp * 2048 + 512 * c + sb:
                         hp * 2048 + 512 * (c + 1)]:
                   e.matmul(o, l, r, start=True, stop=True,
                            tile_position=(64, 0)), deps=[], grp=GB, nopd=True)
        ptsl = pt_s[:, 1024 * par:1024 * (par + 1)]
        i_ex = sop(lambda e, o=ptsl, s=sreg:
                   e.activation(out=o, in_=s[:], func=AF.Exp, scale=0.125,
                                bias=bsm[:, 0:1]),
                   deps=[i_s1, pt_read[par], i_bsm], pdeps=[i_bsm])
        pS_read[par] = i_ex
        dep_av = i_ex
        if b >= 4 * c:
            ll = 128 * (b - 4 * c)
            i_mk0 = vop(lambda e, o=ptsl[:, ll:ll + 128]:
                        e.tensor_mul(o, o, tri_s[:]), deps=[i_ex, i_tri])
            i_mk1 = vop(lambda e, o=ptsl[:, 512 + ll:512 + ll + 128]:
                        e.tensor_mul(o, o, tri_s[:]), deps=[i_mk0])
            dep_av = i_mk1
        return par, ptsl, dep_av

    def emit_av(hp, c, b, nb_, par, ptsl, dep_av):
        n = hp // 2
        vcol = 64 if hp % 2 == 0 else 0
        sb = max(0, 128 * (b - 4 * c))
        vap = vtk[:, n * VST + 128 * b + vcol:n * VST + 128 * b + vcol + 128]
        i_a0 = top(lambda e, o=pU0[:, sb:512], l=vap, r=ptsl[:, sb:512],
                   st=(b == 0), sp=(b == nb_ - 1):
                   e.matmul(o, l, r, start=st, stop=sp),
                   deps=[dep_av] + ([pU_read[0]] if b == 0 else []), grp=GB)
        i_a1 = top(lambda e, o=pU1[:, sb:512], l=vap,
                   r=ptsl[:, 512 + sb:1024],
                   st=(b == 0), sp=(b == nb_ - 1):
                   e.matmul(o, l, r, start=st, stop=sp),
                   deps=([pU_read[1]] if b == 0 else []), grp=GB)
        pt_read[par] = i_a1
        return i_a0, i_a1

    def emit_head(hp, c, av0_last, av1_last):
        l_row = 64 if hp % 2 == 0 else 0
        tpar = (hp * 4 + c) % 2
        u0b, u1b = ub0s[tpar], ub1s[tpar]
        i_u0 = vop(lambda e, d=u0b: e.tensor_copy(d[:], pU0[:]),
                   deps=[av0_last, ub_free[tpar][0]])
        i_u1 = vop(lambda e, d=u1b: e.tensor_copy(d[:], pU1[:]),
                   deps=[av1_last, ub_free[tpar][1]])
        pU_read[0] = i_u0
        pU_read[1] = i_u1
        i_n0 = sop(lambda e, lr=l_row, s=u0b: e.activation(
            out=outb2[lr:lr + 1, 0:512], in_=s[lr:lr + 1, :], func=AF.Ln),
                   deps=[i_u0])
        i_r0 = sop(lambda e, lr=l_row: e.activation(
            out=rbuf[lr:lr + 1, 0:512], in_=outb2[lr:lr + 1, 0:512],
            func=AF.Exp, scale=-1.0), deps=[i_n0])
        i_n1 = sop(lambda e, lr=l_row, s=u1b: e.activation(
            out=outb2[lr:lr + 1, 512:1024], in_=s[lr:lr + 1, :], func=AF.Ln),
                   deps=[i_u1])
        i_r1 = sop(lambda e, lr=l_row: e.activation(
            out=rbuf[lr:lr + 1, 512:1024], in_=outb2[lr:lr + 1, 512:1024],
            func=AF.Exp, scale=-1.0), deps=[i_n1])
        i_c1 = sop(lambda e, lr=l_row: e.mul(rbuf[lr:lr + 1, 512:1024],
                                             rbuf[lr:lr + 1, 512:1024],
                                             lamc[lr:lr + 1, 0:1]),
                   deps=[i_r1], pdeps=[i_lamc])
        i_snA = None
        if debug and hp == 2 and c == 3:
            i_snA = vop(lambda e, lr=l_row: e.tensor_copy(
                dbgb[lr:lr + 1, 0:1024], rbuf[lr:lr + 1, 0:1024]),
                deps=[i_r0, i_c1])
        i_cb = vop(lambda e, lr=l_row: e.tensor_copy(rb_bf[lr:lr + 1, 0:1024],
                                                     rbuf[lr:lr + 1, 0:1024]),
                   deps=[i_r0, i_c1, i_snA])
        return i_u0, i_u1, i_cb

    def emit_rest(hp, c, i_u0, i_u1, i_cb):
        n = hp // 2
        pr_lo = 0 if hp % 2 == 0 else 64
        l_row = 64 if hp % 2 == 0 else 0
        tpar = (hp * 4 + c) % 2
        u0b, u1b = ub0s[tpar], ub1s[tpar]
        i_b0 = top(lambda e, lr=l_row, pl=pr_lo: e.matmul(
            pC[pl:pl + 64, 0:512], one_s[lr:lr + 1, 0:64],
            rb_bf[lr:lr + 1, 0:512], start=True, stop=True),
            deps=[i_cb, pC_read[0]], grp=GB)
        i_b1 = top(lambda e, lr=l_row, pl=pr_lo: e.matmul(
            pC[pl:pl + 64, 512:1024], one_s[lr:lr + 1, 64:128],
            rb_bf[lr:lr + 1, 512:1024], start=True, stop=True), deps=[],
            grp=GB)
        i_ce = vop(lambda e, pl=pr_lo: e.tensor_copy(rbuf[pl:pl + 64, 0:1024],
                                                     pC[pl:pl + 64, 0:1024]),
                   deps=[i_b1])
        pC_read[0] = i_ce
        i_snB = None
        if debug and hp == 2 and c == 3:
            i_snB = vop(lambda e, pl=pr_lo: e.tensor_copy(
                dbgb[pl:pl + 64, 1024:2048], rbuf[pl:pl + 64, 0:1024]),
                deps=[i_ce])
            i_snC = vop(lambda e, s=u0b: e.tensor_copy(dbgb[64:128, 1024:1536],
                                                       s[64:128, :]), deps=[i_ce])
            dma(tap("dbgb", [128, 2048]), dbgb[:], deps=[i_snB, i_snC], ch="tap")
        i_t0 = vop(lambda e, pl=pr_lo, s=u0b: e.tensor_mul(
            t0_s[pl:pl + 64, :], s[pl:pl + 64, :], rbuf[pl:pl + 64, 0:512]),
                   deps=[i_ce, i_snB, i_u0])
        i_t1 = vop(lambda e, pl=pr_lo, s=u1b: e.tensor_mul(
            t1_s[pl:pl + 64, :], s[pl:pl + 64, :], rbuf[pl:pl + 64, 512:1024]),
                   deps=[i_ce, i_u1])
        i_af = vop(lambda e, pl=pr_lo,
                   d=afm[pr_lo:pr_lo + 64,
                         n * 2048 + 512 * c:n * 2048 + 512 * (c + 1)]:
                   e.tensor_sub(d, t0_s[pl:pl + 64, :], t1_s[pl:pl + 64, :]),
                   deps=[i_t0, i_t1])
        ub_free[tpar] = [i_t0, i_t1]

    steps = [(hp_, c_, b_, 4 * c_ + 4)
             for hp_ in range(16) for c_ in range(4) for b_ in range(4 * c_ + 4)]
    pend = None
    pend_rest = [None]

    def close_combine(php, pc, a0, a1):
        if pend_rest[0] is not None:
            emit_rest(*pend_rest[0])
        hd = emit_head(php, pc, a0, a1)
        pend_rest[0] = (php, pc) + hd

    for stp in steps:
        smm = emit_smm(*stp)
        if pend is not None:
            (php, pc, pb, pnb), psmm = pend
            a0, a1 = emit_av(php, pc, pb, pnb, *psmm)
            if pb == pnb - 1:
                close_combine(php, pc, a0, a1)
        pend = (stp, smm)
    (php, pc, pb, pnb), psmm = pend
    a0, a1 = emit_av(php, pc, pb, pnb, *psmm)
    close_combine(php, pc, a0, a1)
    emit_rest(*pend_rest[0])

    barB = P.barrier()
    psB.close()
    if debug:
        dma(tap("afmb", [128, 16384], BF16), afm[:], deps=[barB["sync"]], ch="tap")

    if "C" not in phases:
        P.add("sync", lambda e: e.nop(), deps=[P.tails["sync"]])
        return _finish(nc, P, sems, taps)

    psC = ExitStack()
    pY = [psC.enter_context(nc.psum_tensor("py0", [128, 1024], F32)),
          psC.enter_context(nc.psum_tensor("py1", [128, 1024], F32))]
    pCT = psC.enter_context(nc.psum_tensor("pct", [128, 2048], BF16))

    i_wp = dma(wpstg[:].rearrange("p (k c) -> p k c", k=8), wproj_d,
               deps=[barB["sync"]], ch="w")
    i_wps = vop(lambda e: e.tensor_scalar(wps[:], wpstg[:], ONE_MINUS_LI, None,
                                          ALU.mult), deps=[i_wp, barB["vector"]])
    i_wpp = vop(lambda e: e.tensor_copy(wpp[:], wpstg[:]), deps=[i_wp])

    pY_read = [barB["vector"], barB["vector"]]
    y_ev = []
    for tau in range(NT):
        par = tau % 2
        mm_last = None
        for oc in range(2):
            for k in range(8):
                mm_last = top(
                    lambda e, o=pY[par][:, 512 * oc:512 * (oc + 1)],
                    l=afm[:, 2048 * k + 128 * tau:2048 * k + 128 * (tau + 1)],
                    r=wps[:, 1024 * k + 512 * oc:1024 * k + 512 * (oc + 1)],
                    st=(k == 0), sp=(k == 7):
                    e.matmul(o, l, r, start=st, stop=sp),
                    deps=([i_wps, pY_read[par], barB["tensor"]] if k == 0 else []))
        i_ye = vop(lambda e, d=y_s[:, 1024 * tau:1024 * (tau + 1)], s=pY[par][:]:
                   e.tensor_copy(d, s), deps=[mm_last])
        pY_read[par] = i_ye
        y_ev.append(i_ye)
    if debug:
        dma(tap("yb", [128, 16384], BF16), y_s[:], deps=[y_ev[-1]], ch="tap")

    if "C" in phases and "c1" in phases:
        P.add("sync", lambda e: e.nop(), deps=[P.tails["sync"]])
        return _finish(nc, P, sems, taps)

    xc_free = [None, None]
    x1_free = [None, None]
    pct_read = [None, None]
    ln2_ev = []
    x1dma = []
    x1d_prev = [None, None]
    zsc_free = [None]
    for st in range(NT):
        p_, ut = st // 8, st % 8
        par = st % 2
        mm_last = None
        for oc2 in range(2):
            for kk in range(8):
                mm_last = top(
                    lambda e, o=pY[par][:, 512 * oc2:512 * (oc2 + 1)],
                    l=y_s[:, 1024 * (8 * p_ + kk) + 128 * ut:
                          1024 * (8 * p_ + kk) + 128 * (ut + 1)],
                    r=wpp[:, 1024 * kk + 512 * oc2:1024 * kk + 512 * (oc2 + 1)],
                    st_=(kk == 0), sp=(kk == 7):
                    e.matmul(o, l, r, start=st_, stop=sp),
                    deps=([i_wpp, pY_read[par], y_ev[-1]] if kk == 0 else []))
        rot = st % 2
        i_rs, i_nb, rs, nb = ln_stats(pY[par][:], rot, [mm_last], x2sc[:])
        i_z = sop(lambda e, s=pY[par][:], r=rs, b=nb:
                  e.activation(out=zsc_c[:], in_=s, func=AF.Identity, scale=r,
                               bias=b),
                  deps=[i_rs, i_nb, zsc_free[0]], pdeps=[i_rs, i_nb])
        i_x = dma(xt_c[:, 1024 * rot:1024 * (rot + 1)], xt_v[st],
                  deps=[xc_free[rot], barB["sync"]], ch="x%d" % rot)
        x1sl = x1t_c[:, 1024 * rot:1024 * (rot + 1)]
        i_x1 = vop(lambda e, d=x1sl, xs=xt_c[:, 1024 * rot:1024 * (rot + 1)]:
                   e.tensor_add(d, zsc_c[:], xs),
                   deps=[i_z, i_x, x1_free[rot], x1d_prev[rot]])
        zsc_free[0] = i_x1
        pY_read[par] = i_x1
        xc_free[rot] = i_x1
        i_xd = dma(x1_v[st], x1sl, deps=[i_x1], ch="xo%d" % rot)
        x1dma.append(i_xd)
        x1d_prev[rot] = i_xd
        if "nl2" in phases:
            x1_free[rot] = i_xd
            ln2_ev.append(i_x1)
            continue
        i_rs2, i_nb2, rs2, nb2 = ln_stats(x1sl, rot, [i_x1], x2sc[:])
        i_l2 = sop(lambda e, s=x1sl, r=rs2, b=nb2:
                   e.activation(out=l2t_c[:], in_=s, func=AF.Identity,
                                scale=r, bias=b),
                   deps=[i_rs2, i_nb2, ln2_ev[-1] if ln2_ev else None],
                   pdeps=[i_rs2, i_nb2])
        half = pCT[:, 1024 * (st % 2):1024 * (st % 2 + 1)]
        trs = []
        for k in range(8):
            trs.append(top(lambda e, s=l2t_c[:, 128 * k:128 * (k + 1)],
                           d=half[:, 128 * k:128 * (k + 1)]:
                           e.transpose(d, s, idn_s[:]),
                           deps=([i_l2, pct_read[st % 2]] if k == 0 else []),
                           grp=("t", 0, 0)))
        dst = _mkap(ln2f, 8192 * (st // 8) + 128 * (st % 8), [[1024, 8], [1, 128]])
        srcp = half.rearrange("p (k c) -> p k c", k=8)
        i_l2e = vop(lambda e, d=dst, s=srcp: e.tensor_copy(d, s), deps=[trs[-1]])
        pct_read[st % 2] = i_l2e
        x1_free[rot] = i_l2e
        ln2_ev.append(i_l2e)

    barC = P.barrier()
    psC.close()
    if debug:
        dma(tap("ln2fb", [128, 16384], BF16), ln2f[:], deps=[barC["sync"]], ch="tap")
        dma(tap("x1", [2048, 1024]), x1_d, deps=[barC["sync"]], ch="tap")

    if "D" not in phases:
        P.add("sync", lambda e: e.nop(), deps=[P.tails["sync"]])
        return _finish(nc, P, sems, taps)

    psD = ExitStack()
    pDa = psD.enter_context(nc.psum_tensor("pda", [128, 1024], F32))
    pDb = psD.enter_context(nc.psum_tensor("pdb", [128, 1024], F32))
    pDc = psD.enter_context(nc.psum_tensor("pdc", [128, 1024], F32))
    pDT = psD.enter_context(nc.psum_tensor("pdt", [128, 2048], BF16))

    dstg_free = [None, None, None, None]
    dbf_free = [None, None, None, None]
    slab_i = [0]

    def stream_slab(src_ap, deps=()):
        si = slab_i[0]
        slab_i[0] += 1
        sl = si % 4
        st_ = dstg[:, 1024 * sl:1024 * (sl + 1)]
        bf_ = dbf[:, 1024 * sl:1024 * (sl + 1)]
        i_d = dma(st_.rearrange("p (k c) -> p k c", k=8), src_ap,
                  deps=list(deps) + [dstg_free[sl]], ch="fw%d" % sl)
        i_c = vop(lambda e, d=bf_, s=st_: e.tensor_copy(d, s),
                  deps=[i_d, dbf_free[sl]])
        dstg_free[sl] = i_c
        return bf_, i_c, sl

    scales = {"w1": sclb[:, 0:32], "w2": sclb[:, 32:64], "h1": sclb[:, 64:96],
              "h2": sclb[:, 96:128], "w3": sclb[:, 128:136]}
    ffbs = [ffb0, ffb1]
    out_done = []
    prev_w3_mm = [barC["tensor"]]
    prev_h2_mm = [barC["tensor"]]
    pDT_read = [None, None]
    for tt in range(2):
        l2base = 8192 * tt
        pDa_read = [barC["vector"], barC["vector"]]
        pDb_read = [barC["vector"], barC["vector"]]
        gu_free = [None, None, None, None]
        h0_last = None
        for m in range(32):
            b1, c1_, sl1 = stream_slab(w1_d[m, 0],
                                       deps=[barC["sync"]] if (tt == 0 and m < 2)
                                       else ())
            b2, c2_, sl2 = stream_slab(w2_d[m, 0],
                                       deps=[barC["sync"]] if (tt == 0 and m < 2)
                                       else ())
            mg_last = mu_last = None
            for tc in range(2):
                pi = tc
                mg = mu_ = None
                for k in range(8):
                    mg = top(lambda e, o=pDa[:, 512 * pi:512 * (pi + 1)],
                             l=b1[:, 128 * k:128 * (k + 1)],
                             r=ln2f[:, l2base + 1024 * k + 512 * tc:
                                    l2base + 1024 * k + 512 * (tc + 1)],
                             st=(k == 0), sp=(k == 7):
                             e.matmul(o, l, r, start=st, stop=sp),
                             deps=([c1_, pDa_read[pi], barC["tensor"]]
                                   if k == 0 else []))
                for k in range(8):
                    mu_ = top(lambda e, o=pDb[:, 512 * pi:512 * (pi + 1)],
                              l=b2[:, 128 * k:128 * (k + 1)],
                              r=ln2f[:, l2base + 1024 * k + 512 * tc:
                                     l2base + 1024 * k + 512 * (tc + 1)],
                              st=(k == 0), sp=(k == 7):
                              e.matmul(o, l, r, start=st, stop=sp),
                              deps=([c2_, pDb_read[pi]] if k == 0 else []))
                gq = pt_s[:, 512 * pi:512 * (pi + 1)]
                uq = pt_s[:, 1024 + 512 * pi:1024 + 512 * (pi + 1)]
                i_g = sop(lambda e, o=gq, s=pDa[:, 512 * pi:512 * (pi + 1)],
                          sc=scales["w1"][:, m:m + 1]:
                          e.activation(out=o, in_=s, func=AF.Silu, scale=sc),
                          deps=[mg, gu_free[pi], i_scl], pdeps=[i_scl])
                i_u = sop(lambda e, o=uq, s=pDb[:, 512 * pi:512 * (pi + 1)],
                          sc=scales["w2"][:, m:m + 1]:
                          e.activation(out=o, in_=s, func=AF.Identity, scale=sc),
                          deps=[mu_, gu_free[2 + pi]], pdeps=[i_scl])
                pDa_read[pi] = i_g
                pDb_read[pi] = i_u
                i_h0 = vop(lambda e, d=ha_s[:, 1024 * m + 512 * tc:
                                            1024 * m + 512 * (tc + 1)],
                           g=gq, u=uq: e.tensor_mul(d, g, u),
                           deps=[i_g, i_u] + ([prev_w3_mm[0]]
                                              if (m == 0 and tc == 0) else []))
                gu_free[pi] = i_h0
                gu_free[2 + pi] = i_h0
                h0_last = i_h0
                mg_last, mu_last = mg, mu_
            dbf_free[sl1] = mg_last
            dbf_free[sl2] = mu_last

        def big_layer(w_dram, scl_ap, src, dst, func, nm, nk, dep0, dst_free):
            evs = []
            pr_read = [dep0, dep0]
            last_mm = [None]
            for m in range(nm):
                slabs = []
                for q in range(nk // 8):
                    slabs.append(stream_slab(w_dram[m, q]))
                mmf = None
                for tc in range(2):
                    pi = tc
                    mm = None
                    for k in range(nk):
                        bq, cq, _ = slabs[k // 8]
                        mm = top(lambda e, o=pDc[:, 512 * pi:512 * (pi + 1)],
                                 l=bq[:, 128 * (k % 8):128 * (k % 8 + 1)],
                                 r=src[:, 1024 * k + 512 * tc:
                                       1024 * k + 512 * (tc + 1)],
                                 st=(k == 0), sp=(k == nk - 1):
                                 e.matmul(o, l, r, start=st, stop=sp),
                                 deps=(([cq, pr_read[pi], dep0] if k == 0 else
                                        ([cq] if k % 8 == 0 else []))))
                    i_o = sop(lambda e, o=dst[:, 1024 * m + 512 * tc:
                                              1024 * m + 512 * (tc + 1)],
                              s=pDc[:, 512 * pi:512 * (pi + 1)],
                              sc=scl_ap[:, m:m + 1], f=func:
                              e.activation(out=o, in_=s, func=f, scale=sc),
                              deps=[mm, dst_free], pdeps=[i_scl])
                    pr_read[pi] = i_o
                    evs.append(i_o)
                    mmf = mm
                for _, cq, slq in slabs:
                    dbf_free[slq] = mmf
                last_mm[0] = mmf
            return evs, last_mm[0]

        if debug and tt == 0:
            dma(tap("h0", [128, 32768], BF16), ha_s[:], deps=[h0_last], ch="tap")
        h1_ev, h1_mm = big_layer(h1_d, scales["h1"], ha_s, hb_s, AF.Gelu, 32, 32,
                                 h0_last, prev_h2_mm[0])
        if debug and tt == 0:
            dma(tap("h1", [128, 32768], BF16), hb_s[:], deps=[h1_ev[-1]], ch="tap")
        h2_ev, h2_mm = big_layer(h2_d, scales["h2"], hb_s, ha_s, AF.Gelu, 32, 32,
                                 h1_ev[-1], None)
        prev_h2_mm[0] = h2_mm
        if debug and tt == 0:
            dma(tap("h2", [128, 32768], BF16), ha_s[:], deps=[h2_ev[-1]], ch="tap")
        f_ev, w3_mm = big_layer(w3_d, scales["w3"], ha_s, ffbs[tt], AF.Identity,
                                8, 32, h2_ev[-1], None)
        if debug and tt == 0:
            dma(tap("ff", [128, 8192], BF16), ffbs[tt][:], deps=[f_ev[-1]], ch="tap")
        prev_w3_mm[0] = w3_mm

        xd_free = [None, None]
        ob_free = [None, None]
        x2_free = [None]
        outbs = [outb_a, outb2]
        for ss in range(8):
            st = 8 * tt + ss
            rot = ss % 2
            half = pDT[:, 1024 * rot:1024 * (rot + 1)]
            trs = []
            for k in range(8):
                trs.append(top(lambda e, s=ffbs[tt][:, 1024 * k + 128 * ss:
                                                    1024 * k + 128 * (ss + 1)],
                               d=half[:, 128 * k:128 * (k + 1)]:
                               e.transpose(d, s, idn_s[:]),
                               deps=([f_ev[-1], pDT_read[rot]] if k == 0 else []),
                               grp=("t", 0, 0)))
            i_rs, i_nb, rs, nb = ln_stats(half[:], rot, [trs[-1]], sqbf[:])
            i_z = sop(lambda e, s=half[:], r=rs, b=nb:
                      e.activation(out=x2sc[:], in_=s, func=AF.Identity,
                                   scale=r, bias=b),
                      deps=[i_rs, i_nb, x2_free[0]], pdeps=[i_rs, i_nb])
            xts = [xt_d0, xt_d1][rot]
            i_x = dma(xts[:], x1_v[st],
                      deps=[x1dma[st], xd_free[rot], barC["sync"]], ch="x%d" % rot)
            i_o = vop(lambda e, d=outbs[rot][:], xs=xts[:]:
                      e.tensor_add(d, x2sc[:], xs),
                      deps=[i_z, i_x, ob_free[rot]])
            x2_free[0] = i_o
            pDT_read[rot] = i_o
            xd_free[rot] = i_o
            i_od = dma(out_v[st], outbs[rot][:], deps=[i_o], ch="out%d" % rot)
            ob_free[rot] = i_od
            out_done.append(i_od)

    psD.close()
    P.add("sync", lambda e: e.nop(), deps=out_done)
    return _finish(nc, P, sems, taps)


def _finish(nc, P, sems, taps):
    P.assign()
    hwm = {}
    with nc.Block() as blk:
        @blk.sync
        def _(eng):
            P.emit_engine("sync", eng, sems, hwm)

        @blk.tensor
        def _(eng):
            P.emit_engine("tensor", eng, sems, hwm)

        @blk.vector
        def _(eng):
            P.emit_engine("vector", eng, sems, hwm)

        @blk.scalar
        def _(eng):
            P.emit_engine("scalar", eng, sems, hwm)

        @blk.gpsimd
        def _(eng):
            P.emit_engine("gpsimd", eng, sems, hwm)

    return nc, taps


def _host_layout(inputs):
    f32 = np.float32
    Wqkv = np.ascontiguousarray(np.asarray(inputs["L0_attn_Wqkv"], f32))
    wqkv_t = np.ascontiguousarray(
        Wqkv.reshape(8, 128, 12, 256).transpose(2, 1, 0, 3))
    wproj_t = np.ascontiguousarray(
        np.asarray(inputs["L0_attn_Wproj"], f32).reshape(8, 128, 1024)
        .transpose(1, 0, 2))
    lamv = np.ascontiguousarray(np.stack([
        np.asarray(inputs["L0_attn_lambda_q1"], f32),
        np.asarray(inputs["L0_attn_lambda_k1"], f32),
        np.asarray(inputs["L0_attn_lambda_q2"], f32),
        np.asarray(inputs["L0_attn_lambda_k2"], f32)], axis=0))

    def wtile(w, nm, nk):
        wt = np.asarray(w, f32).T
        t = wt.reshape(nk // 8, 8, 128, nm, 128).transpose(3, 0, 2, 1, 4)
        return np.ascontiguousarray(t)

    w1t = wtile(inputs["L0_ffn_w1_w"], 32, 8)
    w2t = wtile(inputs["L0_ffn_w2_w"], 32, 8)
    h1t = wtile(inputs["L0_ffn_h1_w"], 32, 32)
    h2t = wtile(inputs["L0_ffn_h2_w"], 32, 32)
    w3t = wtile(inputs["L0_ffn_w3_w"], 8, 32)
    scl = np.zeros((128, 136), f32)
    scl[:, 0:32] = np.asarray(inputs["L0_ffn_w1_s"], f32).reshape(32, 128).T
    scl[:, 32:64] = np.asarray(inputs["L0_ffn_w2_s"], f32).reshape(32, 128).T
    scl[:, 64:96] = np.asarray(inputs["L0_ffn_h1_s"], f32).reshape(32, 128).T
    scl[:, 96:128] = np.asarray(inputs["L0_ffn_h2_s"], f32).reshape(32, 128).T
    scl[:, 128:136] = np.asarray(inputs["L0_ffn_w3_s"], f32).reshape(8, 128).T
    return dict(wqkv=wqkv_t, wproj=wproj_t, lamv=lamv, w1t=w1t, w2t=w2t,
                h1t=h1t, h2t=h2t, w3t=w3t, scl=scl)


def kernel(**inputs):
    shared = _host_layout(inputs)
    x = np.asarray(inputs["L0_attn_x"], np.float32)
    nc, _ = _build(debug=False)
    in_maps = []
    for b in range(8):
        xu = np.ascontiguousarray(
            np.concatenate([x[b, 0::2], x[b, 1::2]], axis=0))
        m = {"x": xu}
        m.update(shared)
        in_maps.append(m)
    res = run_bass_kernel_spmd(nc, in_maps, core_ids=list(range(8)))
    out = np.empty((8, 2048, 1024), np.float32)
    for b in range(8):
        o = res.results[b]["out"]
        out[b, 0::2] = o[:1024]
        out[b, 1::2] = o[1024:]
    return out
